# version 67
# baseline (speedup 1.0000x reference)
import math
from contextlib import ExitStack

import numpy as np
import ml_dtypes

import concourse.bass as bass
import concourse.mybir as mybir
from concourse.bass_utils import run_bass_kernel_spmd

F32 = mybir.dt.float32
BF16 = mybir.dt.bfloat16
AF = mybir.ActivationFunctionType
ALU = mybir.AluOpType

D = 1024
NEG = -1.0e30
N_STAGES = 4


def _split_const(v):
    hi = np.float32(v)
    lo = np.float32(np.float64(v) - np.float64(hi))
    return float(hi), float(lo)


class Sem:
    def __init__(self, es, nc, name):
        self.h = es.enter_context(nc.semaphore(name))
        self.v = 0

    def bump(self, n):
        self.v += n
        return n


def build_nc(S=2048, n_stages=N_STAGES):
    nc = bass.Bass("TRN2", target_bir_lowering=False)
    TC = S // 128
    TB = S // 512
    QCW = min(1024, S)
    NQCA = S // QCW

    dr = {}

    def din(name, shape, dt=F32):
        dr[name] = nc.dram_tensor(name, list(shape), dt, kind="ExternalInput").ap()
        return dr[name]

    x_h = din("x", (D, S))
    wa4_h = din("wa4", (D, 1024))
    wv2_h = din("wv2", (D, 2048))
    kvw_h = din("kvw", (D, 2048))
    qw_h = din("qw", (D, 1024))
    pjw_h = din("pjw", (D, 1024))
    ffw_h = din("ffw", (D, 1024))
    din("ba_qk", (128, 8))
    din("bv_row", (1, 2048))
    din("qkvb_qk", (128, 16))
    din("qkvb_v", (1, 1024))
    din("projb", (128, 8))
    din("bk32", (128, 16))
    din("bkc32", (128, 16))
    din("tmat", (64, S))
    din("jcol", (64, 1))
    din("ksin", (64, S))
    din("kcos", (64, S))
    din("stair", (128, 128), BF16)
    din("ident", (128, 128), BF16)
    out_h = nc.dram_tensor("out", [D, S], F32, kind="ExternalOutput").ap()

    es = ExitStack()

    def sb(name, shape, dt):
        return es.enter_context(nc.sbuf_tensor(name, list(shape), dt))

    X = sb("X", (128, 8 * S), F32)
    E = sb("E", (128, 16384), BF16)
    KB = sb("KB", (128, 8 * S), BF16)
    VT = sb("VT", (128, TC * 1040), BF16)
    RC = sb("RC", (128, S), BF16)
    RS = sb("RS", (128, S), BF16)
    F1 = sb("F1", (128, 3072), F32)
    R1 = sb("R1", (128, 4096), BF16)
    QR = sb("QR", (128, 4096), BF16)
    CX = sb("CX", (128, 4096), BF16)
    STAIR = sb("STAIR", (128, 128), BF16)
    IDENT = sb("IDENT", (128, 128), BF16)
    ONESQ = sb("ONESQ", (128, 128), BF16)
    ONESR = sb("ONESR", (1, 128), BF16)
    BIASA = sb("BIASA", (128, 8), F32)
    QKVB = sb("QKVB", (128, 16), F32)
    PROJB = sb("PROJB", (128, 8), F32)
    ZM = sb("ZM", (128, 16), F32)
    ZMI = sb("ZMI", (128, 16), F32)
    QBVB = sb("QBVB", (1, 1024), BF16)
    ONE64 = sb("ONE64", (128, 1), BF16)
    LN16T = sb("LN16T", (128, 1), F32)

    BROWBF = QR[0:1, 0:2048]
    VPMA = QR[0:1, 2048:3088]
    KPMST = QR[:, 3088:3104]
    EPA = QR[0:1, 1024:1024 + QCW]
    VPMBD = CX[64:65, 0:520]
    EPB = CX[64:65, 1024:1024 + QCW]

    PS = es.enter_context(nc.psum_tensor("PS", [128, 4096], F32))

    def bank(i, rows=128):
        return PS[0:rows, 512 * i:512 * (i + 1)]

    class _PB:
        def __init__(self, nc):
            self.nc = nc

        def tensor(self, f):
            f(self.nc.tensor)

        def scalar(self, f):
            f(self.nc.scalar)

        def vector(self, f):
            f(self.nc.vector)

        def gpsimd(self, f):
            f(self.nc.gpsimd)

        def sync(self, f):
            f(self.nc.sync)

    if True:
        block = _PB(nc)
        sT = Sem(es, nc, "sT")
        sS = Sem(es, nc, "sS")
        sV = Sem(es, nc, "sV")
        sG = Sem(es, nc, "sG")
        sX = Sem(es, nc, "sX")
        sW = Sem(es, nc, "sW")
        sZ = Sem(es, nc, "sZ")
        sZK = Sem(es, nc, "sZK")
        sZS = Sem(es, nc, "sZS")
        sZZ = Sem(es, nc, "sZZ")
        sZF = Sem(es, nc, "sZF")
        sZI = Sem(es, nc, "sZI")
        sO = Sem(es, nc, "sO")
        sW2 = Sem(es, nc, "sW2")
        sX2 = Sem(es, nc, "sX2")
        sXB = Sem(es, nc, "sXB")
        sXC = Sem(es, nc, "sXC")
        sXD = Sem(es, nc, "sXD")

        def barrier():
            tv, sv, vv, gv = sT.v, sS.v, sV.v, sG.v

            def bt(t):
                if sv: t.wait_ge(sS.h, sv)
                if vv: t.wait_ge(sV.h, vv)
                if gv: t.wait_ge(sG.h, gv)
            block.tensor(bt)

            def bs(s):
                if tv: s.wait_ge(sT.h, tv)
                if vv: s.wait_ge(sV.h, vv)
                if gv: s.wait_ge(sG.h, gv)
            block.scalar(bs)

            def bv_(v):
                if tv: v.wait_ge(sT.h, tv)
                if sv: v.wait_ge(sS.h, sv)
                if gv: v.wait_ge(sG.h, gv)
            block.vector(bv_)

            def bg(g):
                if tv: g.wait_ge(sT.h, tv)
                if sv: g.wait_ge(sS.h, sv)
                if vv: g.wait_ge(sV.h, vv)
            block.gpsimd(bg)

        x_tb = {}

        def p0_loads(y):
            for name, dst in [("ba_qk", BIASA), ("qkvb_qk", QKVB), ("projb", PROJB),
                              ("bk32", CX[:, 0:32].bitcast(F32)),
                              ("bkc32", CX[:, 32:64].bitcast(F32)),
                              ("stair", STAIR), ("ident", IDENT)]:
                y.dma_start(out=dst[:], in_=dr[name]).then_inc(sX2.h, sX2.bump(16))
            kbf = KB[:].bitcast(F32)
            y.dma_start(out=kbf[0:64, 0:S], in_=dr["tmat"]).then_inc(sX2.h, sX2.bump(16))
            y.dma_start(out=kbf[0:64, S:2 * S], in_=dr["ksin"]).then_inc(sX2.h, sX2.bump(16))
            y.dma_start(out=kbf[0:64, 2 * S:3 * S], in_=dr["kcos"]).then_inc(sX2.h, sX2.bump(16))
            y.dma_start(out=ZM[0:64, 3:4], in_=dr["jcol"]).then_inc(sX2.h, sX2.bump(16))
            y.dma_start(out=F1[0:1, 0:2048], in_=dr["bv_row"]).then_inc(sX2.h, sX2.bump(16))
            y.dma_start(out=F1[0:1, 2048:3072], in_=dr["qkvb_v"]).then_inc(sX2.h, sX2.bump(16))
            for tb in range(TB):
                sl = slice(tb * 512, tb * 512 + 512)
                sem = (sX, sXB, sXC, sXD)[tb % 4]
                y.dma_start(out=X[:].rearrange("p (c t) -> p c t", c=8)[:, :, sl],
                            in_=x_h.rearrange("(c p) t -> p c t", p=128)[:, :, sl]).then_inc(sem.h, sem.bump(16))
                x_tb[tb] = (sem, sem.v)
        block.sync(p0_loads)
        X_LOADED = sX2.v

        wa4_kc = {}

        def p0_w(g):
            for kc in range(8):
                if kc:
                    g.wait_ge(sW.h, sW.v)
                g.dma_start(out=E[:, kc * 1024:(kc + 1) * 1024],
                            in_=wa4_h[kc * 128:(kc + 1) * 128, :]).then_inc(sW.h, sW.bump(16))
                wa4_kc[kc] = sW.v
        block.gpsimd(p0_w)
        WA4_LOADED = sW.v

        def p0_wv_pre(g, pw=WA4_LOADED):
            g.wait_ge(sW.h, pw)
            g.dma_start(out=E[:, 8192:16384].rearrange("p (c o) -> p c o", c=4),
                        in_=wv2_h[0:512, :].rearrange("(c p) o -> p c o", p=128)).then_inc(sW.h, sW.bump(16))
        block.gpsimd(p0_wv_pre)
        WV_PRE = sW.v

        def p0_const(v):
            v.memset(ONESQ[:], 1.0).then_inc(sV.h, sV.bump(1))
            v.wait_ge(sV.h, sV.v)
            v.memset(ONESR[:], 1.0).then_inc(sV.h, sV.bump(1))
            v.wait_ge(sV.h, sV.v)
            v.memset(ONE64[:], 1.0).then_inc(sV.h, sV.bump(1))
            v.wait_ge(sV.h, sV.v)
            v.memset(LN16T[:], math.log(16.0)).then_inc(sV.h, sV.bump(1))
            v.wait_ge(sV.h, sV.v)
            v.memset(VT[:, 64::65], 1.0).then_inc(sV.h, sV.bump(1))
        block.vector(p0_const)

        c_hi, c_lo = _split_const(math.log(10000.0) / 32.0)
        tp_hi, tp_lo = _split_const(2.0 * math.pi)
        hp_hi, hp_lo = _split_const(math.pi / 2.0)
        kbf = KB[:].bitcast(F32)
        TMAT = kbf[0:64, 0:S]
        KSIN = kbf[0:64, S:2 * S]
        KCOS = kbf[0:64, 2 * S:3 * S]
        JCOL = ZM[0:64, 3:4]
        TH = kbf[0:64, 3 * S:4 * S]
        U1 = kbf[0:64, 0:S]
        E1 = ZM[0:64, 0:1]
        EE = ZM[0:64, 1:2]
        FCOL = ZM[0:64, 2:3]

        def rope_v1(v):
            v.wait_ge(sX2.h, X_LOADED)
            v.tensor_scalar_mul(E1, JCOL, -c_hi).then_inc(sV.h, sV.bump(1))
            v.wait_ge(sV.h, sV.v)
            v.scalar_tensor_tensor(EE, JCOL, -c_lo, E1, ALU.mult, ALU.add).then_inc(sV.h, sV.bump(1))
        block.vector(rope_v1)
        ROPE_E = sV.v

        def rope_s1(s):
            s.wait_ge(sV.h, ROPE_E)
            s.activation(FCOL, EE, AF.Exp).then_inc(sS.h, sS.bump(1))
        block.scalar(rope_s1)
        ROPE_F = sS.v

        def rope_v2(v):
            v.wait_ge(sS.h, ROPE_F)
            v.tensor_scalar(TH, TMAT, FCOL, None, ALU.mult).then_inc(sV.h, sV.bump(1))
            v.wait_ge(sV.h, sV.v)
            v.scalar_tensor_tensor(U1, KSIN, -tp_hi, TH, ALU.mult, ALU.add).then_inc(sV.h, sV.bump(1))
            v.wait_ge(sV.h, sV.v)
            v.scalar_tensor_tensor(U1, KSIN, -tp_lo, U1, ALU.mult, ALU.add).then_inc(sV.h, sV.bump(1))
        block.vector(rope_v2)
        ROPE_U = sV.v

        def rope_s2(s):
            s.wait_ge(sV.h, ROPE_U)
            s.activation(RS[0:64, 0:S], U1, AF.Sin).then_inc(sS.h, sS.bump(1))
        block.scalar(rope_s2)
        ROPE_SIN = sS.v

        def rope_v3(v):
            v.wait_ge(sS.h, ROPE_SIN)
            v.tensor_scalar_mul(RS[0:32, 0:S], RS[0:32, 0:S], -1.0).then_inc(sV.h, sV.bump(1))
            v.wait_ge(sV.h, sV.v)
            v.scalar_tensor_tensor(U1, KCOS, -tp_hi, TH, ALU.mult, ALU.add).then_inc(sV.h, sV.bump(1))
            v.wait_ge(sV.h, sV.v)
            v.scalar_tensor_tensor(U1, KCOS, -tp_lo, U1, ALU.mult, ALU.add).then_inc(sV.h, sV.bump(1))
            v.wait_ge(sV.h, sV.v)
            v.tensor_scalar_add(U1, U1, hp_hi).then_inc(sV.h, sV.bump(1))
            v.wait_ge(sV.h, sV.v)
            v.tensor_scalar_add(U1, U1, hp_lo).then_inc(sV.h, sV.bump(1))
        block.vector(rope_v3)
        ROPE_UC = sV.v

        def rope_s3(s):
            s.wait_ge(sV.h, ROPE_UC)
            s.activation(RC[0:64, 0:S], U1, AF.Sin).then_inc(sS.h, sS.bump(1))
        block.scalar(rope_s3)
        ROPE_COS = sS.v

        def rope_g4(y):
            y.wait_ge(sS.h, ROPE_COS)
            y.dma_start(out=RC[64:128, 0:S], in_=RC[0:64, 0:S]).then_inc(sZ.h, sZ.bump(16))
            y.dma_start(out=RS[64:128, 0:S], in_=RS[0:64, 0:S]).then_inc(sZ.h, sZ.bump(16))
        block.sync(rope_g4)
        ROPE_DONE_Z = sZ.v

        def pad_s(s):
            s.wait_ge(sX2.h, X_LOADED)
            s.activation(ZMI[:, 0:16], CX[:, 32:64].bitcast(F32), AF.Sigmoid).then_inc(sS.h, sS.bump(1))
            s.wait_ge(sS.h, sS.v)
            s.activation(R1[0:1, 2048:4096].bitcast(F32), F1[0:1, 1024:2048], AF.Sigmoid).then_inc(sS.h, sS.bump(1))
        block.scalar(pad_s)
        PAD_SIG = sS.v

        def pad_v(v):
            v.wait_ge(sS.h, PAD_SIG)
            v.tensor_mul(KPMST[:], CX[:, 0:32].bitcast(F32), ZMI[:, 0:16]).then_inc(sV.h, sV.bump(1))
            v.tensor_mul(R1[0:1, 2048:4096].bitcast(F32), F1[0:1, 0:1024],
                         R1[0:1, 2048:4096].bitcast(F32)).then_inc(sV.h, sV.bump(1))
            v.wait_ge(sV.h, sV.v)
            stg = R1[0:1, 0:2080]
            v.memset(stg, 0.0).then_inc(sV.h, sV.bump(1))
            v.wait_ge(sV.h, sV.v)
            vp3 = R1[0:1, 2048:4096].bitcast(F32).rearrange("o (p two d) -> o p two d", two=2, d=64)
            v.tensor_copy(R1[0:1, 1040:1560].rearrange("o (p c) -> o p c", c=65)[:, :, 0:64],
                          vp3[:, :, 1, :]).then_inc(sV.h, sV.bump(1))
            v.wait_ge(sV.h, sV.v)
            v.memset(R1[0:1, 1040 + 64:1560:65], 1.0).then_inc(sV.h, sV.bump(1))
            v.wait_ge(sV.h, sV.v)
            v.tensor_scalar_mul(R1[0:1, 2048:4096].bitcast(F32), R1[0:1, 2048:4096].bitcast(F32),
                                16.0).then_inc(sV.h, sV.bump(1))
            v.wait_ge(sV.h, sV.v)
            v.tensor_copy(R1[0:1, 0:1040].rearrange("o (p c) -> o p c", c=130)[:, :, 0:64],
                          vp3[:, :, 0, :]).then_inc(sV.h, sV.bump(1))
            v.wait_ge(sV.h, sV.v)
            v.memset(R1[0:1, 64:1040:130], 16.0).then_inc(sV.h, sV.bump(1))
            v.wait_ge(sV.h, sV.v)
            v.tensor_copy(BROWBF, F1[0:1, 0:2048]).then_inc(sV.h, sV.bump(1))
            v.wait_ge(sV.h, sV.v)
            v.tensor_copy(QBVB[:], F1[0:1, 2048:3072]).then_inc(sV.h, sV.bump(1))
        block.vector(pad_v)
        PADC_V = sV.v
        QBVB_DONE = sV.v

        def vpm_g(y):
            y.wait_ge(sV.h, PADC_V)
            y.dma_start(out=VPMA, in_=R1[0:1, 0:1040]).then_inc(sZ.h, sZ.bump(16))
            y.dma_start(out=VPMBD, in_=R1[0:1, 1040:1560]).then_inc(sZ.h, sZ.bump(16))
        block.sync(vpm_g)
        PADC_DMA = sZ.v
        PADC_DONE = sV.v

        MEAN = F1[:, 2048:2560]
        RSTD = F1[:, 2560:3072]
        T1 = F1[:, 1024:1536]
        T1B = F1[:, 1536:2048]
        XBF = [F1[:, 0:256].bitcast(BF16), F1[:, 512:768].bitcast(BF16)]
        SQB = [F1[:, 256:512].bitcast(BF16), F1[:, 768:1024].bitcast(BF16)]
        XBF1 = [F1[:, 0:256].bitcast(BF16)] * 2
        SQB1 = [F1[:, 256:512].bitcast(BF16)] * 2
        last_stats_mm = [0]

        def ln_chunk(c, toff, p3, wait_t, sqw_v=0, sq_on_v=False):
            xbf_l = XBF1 if p3 else XBF
            sqb_l = SQB1 if p3 else SQB
            xb, sq = xbf_l[c % 2], sqb_l[c % 2]
            xsl = X[:, c * S + toff:c * S + toff + 512]

            def cast_fn(v, xb=xb, sq=sq, xsl=xsl, wait_t=wait_t, sqw_v=sqw_v):
                if wait_t:
                    v.wait_ge(sT.h, wait_t)
                if sqw_v:
                    v.wait_ge(sV.h, sqw_v)
                v.tensor_copy(xb, xsl).then_inc(sV.h, sV.bump(1))
                if sq_on_v:
                    v.wait_ge(sV.h, sV.v)
                    v.tensor_mul(sq, xb, xb).then_inc(sV.h, sV.bump(1))
            block.vector(cast_fn)
            vtgt = sV.v

            if not sq_on_v:
                def sq_fn(s, sq=sq, xsl=xsl, wait_t=wait_t, sqw_v=sqw_v):
                    if wait_t:
                        s.wait_ge(sT.h, wait_t)
                    if sqw_v:
                        s.wait_ge(sV.h, sqw_v)
                    s.activation(sq, xsl, AF.Square).then_inc(sS.h, sS.bump(1))
                block.scalar(sq_fn)
            stgt = sS.v if not sq_on_v else 0

            def mm_fn(t, xb=xb, sq=sq, c=c, vtgt=vtgt, stgt=stgt):
                t.wait_ge(sV.h, vtgt)
                if stgt:
                    t.wait_ge(sS.h, stgt)
                mi = t.matmul(bank(0), ONESQ[:], xb, start=(c == 0), stop=(c == 7))
                mi = t.matmul(bank(1), ONESQ[:], sq, start=(c == 0), stop=(c == 7)).then_inc(sT.h, sT.bump(1))
            block.tensor(mm_fn)
            return sT.v

        def emit_ln(toff, ydst_fn, extra_wait_t=0, p3=False):
            back = 1 if p3 else 2
            t_done_mm = []
            for c in range(8):
                wait_t = (t_done_mm[c - back] if c >= back else max(last_stats_mm[0], extra_wait_t))
                t_done_mm.append(ln_chunk(c, toff, p3, wait_t))
            last_stats_mm[0] = t_done_mm[-1]
            return ln_finish(toff, ydst_fn, t_done_mm[-1], extra_wait_t, p3)[-1]

        def ln_finish(toff, ydst_fn, STATS_MM, extra_wait_t=0, p3=False):
            t1 = T1B if p3 else T1

            def stat_v(v, STATS_MM=STATS_MM):
                v.wait_ge(sT.h, STATS_MM)
                v.tensor_scalar_mul(MEAN, bank(0), 1.0 / 1024.0).then_inc(sV.h, sV.bump(1))
                v.wait_ge(sV.h, sV.v)
                v.tensor_mul(RSTD, MEAN, MEAN).then_inc(sV.h, sV.bump(1))
                v.wait_ge(sV.h, sV.v)
                v.scalar_tensor_tensor(RSTD, bank(1), 1.0 / 1024.0, RSTD,
                                       ALU.mult, ALU.subtract).then_inc(sV.h, sV.bump(1))
                v.wait_ge(sV.h, sV.v)
                v.tensor_scalar_add(RSTD, RSTD, 1e-5).then_inc(sV.h, sV.bump(1))
                v.wait_ge(sV.h, sV.v)
            block.vector(stat_v)
            VAR_DONE = sV.v

            def stat_s(s):
                s.wait_ge(sV.h, VAR_DONE)
                s.activation(RSTD, RSTD, AF.Ln).then_inc(sS.h, sS.bump(1))
                s.wait_ge(sS.h, sS.v)
                s.activation(RSTD, RSTD, AF.Exp, scale=-0.5).then_inc(sS.h, sS.bump(1))
            block.scalar(stat_s)
            RSTD_DONE = sS.v

            app_vals = []
            for c in range(8):
                xsl = X[:, c * S + toff:c * S + toff + 512]

                def app_v(v, xsl=xsl, c=c, dst=ydst_fn(c), ew=extra_wait_t, t1=t1):
                    if c == 0:
                        v.wait_ge(sS.h, RSTD_DONE)
                        if ew:
                            v.wait_ge(sT.h, ew)
                    v.scalar_tensor_tensor(t1, MEAN, -1.0, xsl, ALU.mult, ALU.add).then_inc(sV.h, sV.bump(1))
                    v.wait_ge(sV.h, sV.v)
                    v.tensor_mul(dst, t1, RSTD).then_inc(sV.h, sV.bump(1))
                    v.wait_ge(sV.h, sV.v)
                block.vector(app_v)
                app_vals.append(sV.v)
            return app_vals

        def ydst(c):
            return R1[:, c * 512:(c + 1) * 512]

        QA_OFF = 4 * S
        KD_OFF = 6 * S

        if n_stages >= 1:
            def p1a_gate(t):
                t.wait_ge(x_tb[0][0].h, x_tb[0][1])
            block.tensor(p1a_gate)

            def p1a_gate_v(v):
                v.wait_ge(sZ.h, PADC_DMA)
                v.wait_ge(x_tb[0][0].h, x_tb[0][1])
            block.vector(p1a_gate_v)

            def p1a_gate_s(s):
                s.wait_ge(sV.h, PADC_V)
                s.wait_ge(x_tb[0][0].h, x_tb[0][1])
            block.scalar(p1a_gate_s)

            ev_hist = [0, 0]
            sig_hist = [0, 0]
            unit_i = 0
            NCACHE = min(TB, (4 * S) // 4096)
            cache_first = TB - NCACHE
            p1_apps = {}

            def yslice(tb, lo, hi):
                if tb >= cache_first:
                    base = (tb - cache_first) * 4096
                    return KB[:, base + lo:base + hi]
                return R1[:, lo:hi]

            for tb in range(TB):
                if tb > 0:
                    def xg_v(v, xs=x_tb[tb][0], xv=x_tb[tb][1]):
                        v.wait_ge(xs.h, xv)
                    block.vector(xg_v)

                    def xg_s(s, xs=x_tb[tb][0], xv=x_tb[tb][1]):
                        s.wait_ge(xs.h, xv)
                    block.scalar(xg_s)
                yd = (lambda c, tb=tb: yslice(tb, c * 512, (c + 1) * 512))
                ln_done = emit_ln(tb * 512, yd)
                p1_apps[tb] = ln_done
                for pi, (o_raw, o_gate, b_raw, b_gate, dst_off) in enumerate([
                        (0, 256, 0, 2, QA_OFF),
                        (512, 768, 4, 6, KD_OFF)]):
                    for oc in range(2):
                        br = 2 + 2 * (unit_i % 2)
                        bg = br + 1

                        def mm_fn(t, o_raw=o_raw, o_gate=o_gate, oc=oc, br=br, bg=bg, tb=tb,
                                  vt=ln_done, first=(tb == 0 and unit_i == 0),
                                  lev=ev_hist[0], lsig=sig_hist[0]):
                            t.wait_ge(sV.h, max(vt, lev))
                            if lsig:
                                t.wait_ge(sS.h, lsig)
                            for kc in range(8):
                                if first:
                                    t.wait_ge(sW.h, wa4_kc[kc])
                                mi = t.matmul(bank(br), E[:, kc * 1024 + o_raw + oc * 128:kc * 1024 + o_raw + oc * 128 + 128],
                                         yslice(tb, kc * 512, (kc + 1) * 512), start=(kc == 0), stop=(kc == 7))
                            for kc in range(8):
                                mi = t.matmul(bank(bg), E[:, kc * 1024 + o_gate + oc * 128:kc * 1024 + o_gate + oc * 128 + 128],
                                         yslice(tb, kc * 512, (kc + 1) * 512), start=(kc == 0), stop=(kc == 7))
                            mi.then_inc(sT.h, sT.bump(1))
                        block.tensor(mm_fn)
                        MM_DONE = sT.v
                        GT = F1[:, 1536:2048]
                        bgt = BIASA[:, b_gate + oc:b_gate + oc + 1]
                        braw = BIASA[:, b_raw + oc:b_raw + oc + 1]

                        def sig_fn(s, bgt=bgt, bg=bg, tgt=MM_DONE, lev=ev_hist[1]):
                            s.wait_ge(sT.h, tgt)
                            if lev:
                                s.wait_ge(sV.h, lev)
                            s.activation(GT, bank(bg), AF.Sigmoid, bias=bgt).then_inc(sS.h, sS.bump(1))
                        block.scalar(sig_fn)
                        sig_hist = [sig_hist[1], sS.v]

                        dst = KB[:, dst_off + oc * S + tb * 512: dst_off + oc * S + tb * 512 + 512]

                        def ev_fn(v, braw=braw, br=br, dst=dst, tgt=sS.v):
                            v.wait_ge(sS.h, tgt)
                            v.scalar_tensor_tensor(dst, bank(br), braw, GT, ALU.add, ALU.mult).then_inc(sV.h, sV.bump(1))
                        block.vector(ev_fn)
                        ev_hist = [ev_hist[1], sV.v]
                        unit_i += 1
            P1A_V = sV.v
            P1A_T = sT.v

            def wv_g(g, tt=P1A_T, pw=WV_PRE):
                g.wait_ge(sT.h, tt)
                g.wait_ge(sW.h, pw)
                g.dma_start(out=E[:, 0:8192].rearrange("p (c o) -> p c o", c=4),
                            in_=wv2_h[512:1024, :].rearrange("(c p) o -> p c o", p=128)).then_inc(sW.h, sW.bump(16))
            block.gpsimd(wv_g)
            WV_LOADED = sW.v

            def wsl(c, off):
                base = 8192 + c * 2048 if c < 4 else (c - 4) * 2048
                return E[:, base + off:base + off + 512]

            vev_hist = [0, 0]
            vsig_hist = [0, 0]
            unit_v = 0
            vev_tc = {}
            kred_ch = {}
            for tb in range(TB):
                if tb >= cache_first:
                    ln_done = p1_apps[tb]
                else:
                    ln_done = emit_ln(tb * 512, ydst)
                for u in range(4):
                    tc = tb * 4 + u
                    for oh in range(2):
                        br = 2 + 2 * (unit_v % 2)
                        bg = br + 1

                        first = (tb == 0 and u == 0 and oh == 0)

                        def vmm_fn(t, u=u, oh=oh, br=br, bg=bg, tb=tb, vt=ln_done,
                                   first=first, lv=vev_hist[0], ls=vsig_hist[0]):
                            t.wait_ge(sV.h, max(vt, lv))
                            if ls:
                                t.wait_ge(sS.h, ls)
                            for c in range(8):
                                if first:
                                    if c == 0:
                                        t.wait_ge(sW.h, WV_PRE)
                                    elif c == 4:
                                        t.wait_ge(sW.h, WV_LOADED)
                                mi = t.matmul(bank(br), yslice(tb, c * 512 + u * 128, c * 512 + u * 128 + 128),
                                         wsl(c, oh * 512),
                                         start=(c == 0), stop=False)
                            mi = t.matmul(bank(br), ONESR[:], BROWBF[:, oh * 512:oh * 512 + 512],
                                     start=False, stop=True)
                            for c in range(8):
                                mi = t.matmul(bank(bg), yslice(tb, c * 512 + u * 128, c * 512 + u * 128 + 128),
                                         wsl(c, 1024 + oh * 512),
                                         start=(c == 0), stop=False)
                            mi = t.matmul(bank(bg), ONESR[:], BROWBF[:, 1024 + oh * 512:1024 + oh * 512 + 512],
                                     start=False, stop=True)
                            mi.then_inc(sT.h, sT.bump(1))
                        block.tensor(vmm_fn)
                        VMM = sT.v
                        GT = F1[:, 1536:2048]

                        def vsig_fn(s, tgt=VMM, bg=bg, lv=vev_hist[1]):
                            s.wait_ge(sT.h, tgt)
                            if lv:
                                s.wait_ge(sV.h, lv)
                            s.activation(GT, bank(bg), AF.Sigmoid).then_inc(sS.h, sS.bump(1))
                        block.scalar(vsig_fn)
                        vsig_hist = [vsig_hist[1], sS.v]

                        def vev_fn(v, tc=tc, oh=oh, br=br, tgt=sS.v):
                            v.wait_ge(sS.h, tgt)
                            dst = VT[:, tc * 1040 + oh * 520:tc * 1040 + oh * 520 + 520]
                            dst = dst.rearrange("p (h c) -> p h c", c=65)[:, :, 0:64]
                            v.tensor_tensor(dst, bank(br), GT, ALU.mult).then_inc(sV.h, sV.bump(1))
                        block.vector(vev_fn)
                        vev_hist = [vev_hist[1], sV.v]
                        vev_tc[tc] = sV.v
                        unit_v += 1
                if NCACHE == 0 and tb == 0:
                    lohi = (0, 4 * S)
                elif tb >= cache_first:
                    lo = (tb - cache_first) * 4096
                    lohi = (lo, min(lo + 4096, 4 * S))
                else:
                    lohi = None
                if lohi is not None:
                    lo, hi = lohi

                    def kz_v(v, tt=sT.v, zd=PADC_DMA, lo=lo, hi=hi, first=(lo == 0)):
                        v.wait_ge(sT.h, tt)
                        if first:
                            v.wait_ge(sZ.h, zd)
                        v.memset(KB[:, lo:hi], 0.0).then_inc(sV.h, sV.bump(1))
                    block.vector(kz_v)
                    KZ_DONE = sV.v

                    def kred_pair(h):
                        src = KB[16 * (h % 8):16 * (h % 8) + 16,
                                 KD_OFF + (h // 8) * S:KD_OFF + (h // 8) * S + S]
                        g = h // 2
                        ch = g // 2
                        r0 = 64 * (h % 2) + 32 * (g % 2) + 16 * ((g // 2) % 2)
                        return KB[r0:r0 + 16, ch * S:(ch + 1) * S], src

                    heads_tb = [h for h in range(16) if lo <= ((h // 2) // 2) * S < hi]
                    ksem = (sX, sXB, sXC, sXD)[tb % 4]

                    def kred_y(y, kz=KZ_DONE, hs=tuple(heads_tb[0::2]), ksem=ksem):
                        y.wait_ge(sV.h, kz)
                        for h in hs:
                            dst, src = kred_pair(h)
                            y.dma_start(out=dst, in_=src).then_inc(ksem.h, ksem.bump(16))
                    block.sync(kred_y)

                    def kred_s(s, kz=KZ_DONE, hs=tuple(heads_tb[1::2]), ksem=ksem):
                        s.wait_ge(sV.h, kz)
                        for h in hs:
                            dst, src = kred_pair(h)
                            s.dma_start(out=dst, in_=src).then_inc(ksem.h, ksem.bump(16))
                    block.scalar(kred_s)
                    for ch in range(lo // S, max(lo // S + 1, hi // S)):
                        kred_ch[ch] = (ksem, ksem.v)
            KRED_ALL = [(sm, sm.v) for sm in (sX, sXB, sXC, sXD)]
            P1B_T = sT.v

        if n_stages >= 1:
            P2_V0 = sV.v
            P2_S0 = sS.v

            if n_stages >= 2:
                def kv_prefetch(g, tt=P1B_T, pw=sW.v):
                    g.wait_ge(sT.h, tt)
                    g.wait_ge(sW.h, pw)
                    g.dma_start(out=E[:, 0:16384].rearrange("p (c o) -> p c o", c=8),
                                in_=kvw_h.rearrange("(c p) o -> p c o", p=128)).then_inc(sW.h, sW.bump(16))
                block.gpsimd(kv_prefetch)

            CTa = F1[0:65, 0:QCW]
            CTb = F1[0:65, QCW:2 * QCW]
            if 2 * QCW <= 1024:
                ZRI = F1[0:1, 2048:2048 + QCW].bitcast(BF16)
            else:
                ZRI = KB[0:1, 6 * S:6 * S + 2 * QCW]
            NH = QCW // 512
            exp_done = {}
            ctx_done = {}
            pending_tail = [None]
            prev_nrm_v = P2_V0
            prev_exp_last = P2_S0
            prev_ctx_all = 0
            prev_zb_t = 0
            prev_xadd_g = 0
            P2_MARK = {}
            for qc in range(NQCA):
                for gidx in range(8):
                    if qc == NQCA - 1 and gidx == 1:
                        P2_MARK["g"] = sG.v
                        P2_MARK["zz"] = sZZ.v
                    h0, h1 = 2 * gidx, 2 * gidx + 1
                    qa_ch = gidx // 4

                    def pad_mm(t, gidx=gidx, qc=qc, qa_ch=qa_ch,
                               pe=prev_exp_last, pc=max(PADC_DONE, P2_V0), pd=PADC_DMA):
                        if pe:
                            t.wait_ge(sS.h, pe)
                        t.wait_ge(sV.h, pc)
                        t.wait_ge(sZ.h, pd)
                        for half in range(NH):
                            for p in range(2):
                                base = 64 * p + 32 * (gidx % 2)
                                rhs = KB[base:base + 32,
                                         QA_OFF + qa_ch * S + qc * QCW + half * 512:
                                         QA_OFF + qa_ch * S + qc * QCW + half * 512 + 512]
                                mi = t.matmul(PS[0:1, p * 1024 + half * 512:p * 1024 + half * 512 + 512],
                                              KPMST[base:base + 32, 2 * gidx + p:2 * gidx + p + 1], rhs,
                                              start=True, stop=True, tile_position=(base, 0))
                        mi.then_inc(sT.h, sT.bump(1))
                    block.tensor(pad_mm)
                    PADMM = sT.v

                    def pad_exp(s, tgt=PADMM, pca=prev_ctx_all):
                        s.wait_ge(sT.h, tgt)
                        if pca:
                            s.wait_ge(sT.h, pca)
                        s.activation(EPA, PS[0:1, 0:QCW], AF.Exp, scale=0.125).then_inc(sS.h, sS.bump(1))
                        s.activation(EPB, PS[0:1, 1024:1024 + QCW], AF.Exp,
                                     bias=LN16T[64:65, 0:1], scale=0.125).then_inc(sS.h, sS.bump(1))
                    block.scalar(pad_exp)
                    EP_DONE = sS.v

                    def emit_scores(tc, hi_, h):
                        ksl_ch = gidx // 2
                        base = 64 * hi_ + 32 * (gidx % 2)
                        wv = exp_done.get((gidx, tc - 1, hi_), EP_DONE)

                        def sc(t, tc=tc, hi_=hi_, base=base, ksl_ch=ksl_ch, wv=wv,
                               qa_ch=qa_ch, qc=qc, kr=(kred_ch[ksl_ch] if qc == 0 and tc == 0 else None),
                               v0=(P2_V0 if qc == 0 and tc == 0 else 0)):
                            t.wait_ge(sS.h, wv)
                            if kr is not None:
                                t.wait_ge(kr[0].h, kr[1])
                            if v0:
                                t.wait_ge(sV.h, v0)
                            lhsT = KB[base:base + 32, ksl_ch * S + tc * 128:ksl_ch * S + tc * 128 + 128]
                            for half in range(NH):
                                rhs = KB[base:base + 32,
                                         QA_OFF + qa_ch * S + qc * QCW + half * 512:
                                         QA_OFF + qa_ch * S + qc * QCW + half * 512 + 512]
                                mi = t.matmul(bank(hi_ * NH + half), lhsT, rhs,
                                              start=True, stop=True, tile_position=(base, 0))
                            mi.then_inc(sT.h, sT.bump(1))
                        block.tensor(sc)
                        return sT.v

                    def emit_exp(tc, hi_):
                        buf = tc % 2
                        sct = sc_done[(tc, hi_)]
                        bufree = ctx_done.get((gidx, tc - 2, hi_), prev_ctx_all)

                        def ex(s, tc=tc, hi_=hi_, buf=buf, sct=sct, bufree=bufree):
                            s.wait_ge(sT.h, max(sct, bufree))
                            s.activation(R1[:, buf * 2048 + hi_ * QCW:buf * 2048 + hi_ * QCW + QCW],
                                         PS[:, hi_ * QCW:hi_ * QCW + QCW],
                                         AF.Exp, scale=0.125).then_inc(sS.h, sS.bump(1))
                        block.scalar(ex)
                        exp_done[(gidx, tc, hi_)] = sS.v
                        return sS.v

                    def emit_ctx(tc, hi_, h):
                        buf = tc % 2
                        ev = exp_done[(gidx, tc, hi_)]
                        pw = prev_nrm_v if tc == 0 else 0

                        def cx(t, tc=tc, hi_=hi_, h=h, buf=buf, ev=ev, pw=pw,
                               vt=(vev_tc[tc] if qc == 0 else 0)):
                            t.wait_ge(sS.h, ev)
                            if pw:
                                t.wait_ge(sV.h, pw)
                            if vt:
                                t.wait_ge(sV.h, vt)
                            lhsT = VT[:, tc * 1040 + 65 * h:tc * 1040 + 65 * h + 65]
                            for half in range(NH):
                                mi = t.matmul(PS[0:65, 2048 + hi_ * QCW + half * 512:2048 + hi_ * QCW + half * 512 + 512],
                                              lhsT,
                                              R1[:, buf * 2048 + hi_ * QCW + half * 512:buf * 2048 + hi_ * QCW + half * 512 + 512],
                                              start=(tc == 0), stop=False)
                            mi.then_inc(sT.h, sT.bump(1))
                        block.tensor(cx)
                        ctx_done[(gidx, tc, hi_)] = sT.v

                    sc_done = {}
                    for tc in range(TC):
                        for hi_, h in ((0, h0), (1, h1)):
                            sc_done[(tc, hi_)] = emit_scores(tc, hi_, h)
                        for hi_ in (0, 1):
                            emit_exp(tc, hi_)
                        if tc == 0 and pending_tail[0] is not None:
                            pending_tail[0]()
                            pending_tail[0] = None
                        if tc > 0:
                            for hi_, h in ((0, h0), (1, h1)):
                                emit_ctx(tc - 1, hi_, h)
                    for hi_, h in ((0, h0), (1, h1)):
                        emit_ctx(TC - 1, hi_, h)
                    prev_exp_last = exp_done[(gidx, TC - 1, 1)]

                    def padctx_fn(t, gidx=gidx):
                        for half in range(NH):
                            mi = t.matmul(PS[0:65, 2048 + half * 512:2048 + half * 512 + 512],
                                          VPMA[:, 130 * gidx:130 * gidx + 65],
                                          EPA[:, half * 512:half * 512 + 512],
                                          start=False, stop=True)
                        for half in range(NH):
                            mi = t.matmul(PS[0:65, 2048 + QCW + half * 512:2048 + QCW + half * 512 + 512],
                                          VPMBD[:, 65 * gidx:65 * gidx + 65],
                                          EPB[:, half * 512:half * 512 + 512],
                                          start=False, stop=True)
                        mi.then_inc(sT.h, sT.bump(1))
                    block.tensor(padctx_fn)
                    prev_ctx_all = sT.v
                    CTX_ALL = sT.v

                    def evac_v(v, tgt=CTX_ALL, pvv=sV.v):
                        v.wait_ge(sT.h, tgt)
                        v.wait_ge(sV.h, pvv)
                        v.tensor_copy(F1[0:65, 0:2 * QCW], PS[0:65, 2048:2048 + 2 * QCW]).then_inc(sV.h, sV.bump(1))
                    block.vector(evac_v)
                    EVAC = sV.v

                    nzm = 2 * QCW // 128

                    def zm_g(y, tgt=EVAC):
                        y.wait_ge(sV.h, tgt)
                        y.dma_start(out=ZM[:, 0:nzm], in_=F1[64:65, 0:2 * QCW]).then_inc(sZZ.h, sZZ.bump(16))
                    block.sync(zm_g)
                    ZMD = sZZ.v

                    def recip_v(v, tgt=ZMD):
                        v.wait_ge(sZZ.h, tgt)
                        v.reciprocal(ZMI[:, 0:nzm], ZM[:, 0:nzm]).then_inc(sV.h, sV.bump(1))
                    block.vector(recip_v)
                    RECIP = sV.v

                    def zri_g(g, tgt=RECIP, pzb=prev_zb_t,
                              kr=(tuple(KRED_ALL) if qc == 0 and gidx == 0 else None)):
                        g.wait_ge(sV.h, tgt)
                        if kr is not None:
                            for sm, vv in kr:
                                if vv:
                                    g.wait_ge(sm.h, vv)
                        if pzb:
                            g.wait_ge(sT.h, pzb)
                        g.dma_start(out=ZRI, in_=ZMI[:, 0:nzm]).then_inc(sZI.h, sZI.bump(16))
                    block.gpsimd(zri_g)
                    ZRID = sZI.v

                    def make_tail(ZRID=ZRID, gidx=gidx, qc=qc):
                        def emit_tail():
                            nonlocal prev_nrm_v, prev_zb_t, prev_xadd_g
                            units = [(hi_, half) for hi_ in range(2) for half in range(NH)]
                            nrm_vals = []
                            for ui, (hi_, half) in enumerate(units):
                                zb_b = 4 + ui % 2
                                bank_free = nrm_vals[ui - 2] if ui >= 2 else 0

                                def zb_fn(t, hi_=hi_, half=half, zb_b=zb_b, tgt=ZRID, pn=bank_free):
                                    t.wait_ge(sZI.h, tgt)
                                    if pn:
                                        t.wait_ge(sV.h, pn)
                                    mi = t.matmul(bank(zb_b, rows=64), ONESR[0:1, 0:64],
                                                  ZRI[0:1, hi_ * QCW + half * 512:hi_ * QCW + half * 512 + 512],
                                                  start=True, stop=True)
                                    mi.then_inc(sT.h, sT.bump(1))
                                block.tensor(zb_fn)
                                ZB = sT.v
                                prev_zb_t = ZB

                                def nrm_fn(v, hi_=hi_, half=half, zb_b=zb_b, ui=ui, tgt=ZB, pg=prev_xadd_g):
                                    v.wait_ge(sT.h, tgt)
                                    if pg:
                                        v.wait_ge(sG.h, pg)
                                    ct = F1[0:64, hi_ * QCW + half * 512:hi_ * QCW + half * 512 + 512]
                                    v.tensor_tensor(CX[0:64, (ui % 4) * 512:(ui % 4) * 512 + 512],
                                                    bank(zb_b, rows=64), ct, ALU.mult).then_inc(sV.h, sV.bump(1))
                                block.vector(nrm_fn)
                                NRM = sV.v
                                prev_nrm_v = NRM
                                nrm_vals.append(NRM)

                                h_ = 2 * gidx + hi_
                                if h_ % 2 == 1:
                                    def shift_g(y2, ui=ui, tgt=NRM):
                                        y2.wait_ge(sV.h, tgt)
                                        y2.dma_start(out=CX[64:128, 2048 + (ui % 4) * 512:2048 + (ui % 4) * 512 + 512],
                                                     in_=CX[0:64, (ui % 4) * 512:(ui % 4) * 512 + 512]).then_inc(sZF.h, sZF.bump(16))
                                    block.sync(shift_g)
                                    SHF = sZF.v

                                    def xadd_fn(g2, half=half, gidx=gidx, qc=qc, ui=ui, h=h_, tgt=SHF):
                                        g2.wait_ge(sZF.h, tgt)
                                        xsl = X[64:128, (h // 2) * S + qc * QCW + half * 512:(h // 2) * S + qc * QCW + half * 512 + 512]
                                        cxs = CX[64:128, 2048 + (ui % 4) * 512:2048 + (ui % 4) * 512 + 512]
                                        g2.tensor_add(xsl, xsl, cxs).then_inc(sG.h, sG.bump(1))
                                    block.gpsimd(xadd_fn)
                                else:
                                    def xadd_fn(g2, half=half, gidx=gidx, qc=qc, ui=ui, h=h_, tgt=NRM):
                                        g2.wait_ge(sV.h, tgt)
                                        xsl = X[0:64, (h // 2) * S + qc * QCW + half * 512:(h // 2) * S + qc * QCW + half * 512 + 512]
                                        cxs = CX[0:64, (ui % 4) * 512:(ui % 4) * 512 + 512]
                                        g2.tensor_add(xsl, xsl, cxs).then_inc(sG.h, sG.bump(1))
                                    block.gpsimd(xadd_fn)
                                prev_xadd_g = sG.v
                        return emit_tail

                    pending_tail[0] = make_tail()
            P2_QR_FREE = sT.v
            if pending_tail[0] is not None:
                pending_tail[0]()
                pending_tail[0] = None

            if n_stages >= 2:
                LAST_EXP = sS.v
                LAST_CTX = prev_ctx_all

                def ln20_gate_v(v, gg=(P2_MARK["g"] if NQCA > 1 else sG.v), zz=sZZ.v):
                    v.wait_ge(sG.h, gg)
                    v.wait_ge(sZZ.h, zz)
                block.vector(ln20_gate_v)

                def ln20_gate_t(t, se=LAST_EXP):
                    t.wait_ge(sS.h, se)
                block.tensor(ln20_gate_t)

                ln2_chunks = []
                for c in range(8):
                    wt = ln2_chunks[c - 1] if c >= 1 else last_stats_mm[0]
                    ln2_chunks.append(ln_chunk(c, 0, True, wt, sq_on_v=True))
                last_stats_mm[0] = ln2_chunks[-1]
                ln2_apps = ln_finish(0, ydst, ln2_chunks[-1],
                                     extra_wait_t=LAST_CTX, p3=True)

        if n_stages >= 2:
            def kv_load(g, gg=sG.v, tt=P2_QR_FREE, pw=sW.v):
                if tt:
                    g.wait_ge(sT.h, tt)
                g.dma_start(out=QR[:].rearrange("p (c o) -> p c o", c=4),
                            in_=ffw_h[0:512, :].rearrange("(c p) o -> p c o", p=128)).then_inc(sW2.h, sW2.bump(16))
                if gg:
                    g.wait_ge(sG.h, gg)
                g.wait_ge(sW.h, pw)
                g.dma_start(out=CX[:].rearrange("p (c o) -> p c o", c=4),
                            in_=ffw_h[512:1024, :].rearrange("(c p) o -> p c o", p=128)).then_inc(sW.h, sW.bump(16))
            block.gpsimd(kv_load)
            KV_QR = sW2.v
            KV_LOADED = sW.v

            sign_vals = []
            for sc_ in range(8):
                def ff_sign(s, sc_=sc_, tgt=(KV_QR if sc_ < 4 else KV_LOADED)):
                    if sc_ < 4:
                        s.wait_ge(sW2.h, tgt)
                    else:
                        s.wait_ge(sW.h, tgt)
                    base = QR if sc_ < 4 else CX
                    k = sc_ % 4
                    s.activation(base[:, k * 1024:k * 1024 + 1024],
                                 base[:, k * 1024:k * 1024 + 1024], AF.Sign).then_inc(sS.h, sS.bump(1))
                block.scalar(ff_sign)
                sign_vals.append(sS.v)
            FF_SIGN = sS.v

            barrier()

            def fslice(kc, oc):
                base = QR if kc < 4 else CX
                k = kc % 4
                return base[:, k * 1024 + oc * 128:k * 1024 + oc * 128 + 128]

            KRAW = F1[:, 1024:1536]
            KSW = F1[:, 512:1024]
            fev_hist = [0, 0]
            krot_hist = [0, 0]
            vevb_hist = [0, 0]
            unit_vb = 0
            kv_live = (n_stages >= 3)
            for tb in range(TB):
                ln3_chunks = []
                fevs = []
                for oc in range(8):
                    b = 2 + (oc % 2)

                    def fmm(t, oc=oc, b=b, apps=tuple(ln2_apps), sgn=tuple(sign_vals),
                            lf=fev_hist[0], tb_=tb):
                        if lf:
                            t.wait_ge(sV.h, lf)
                        for kc in range(8):
                            if oc == 0:
                                if tb_ == 0:
                                    t.wait_ge(sS.h, sgn[kc])
                                t.wait_ge(sV.h, apps[kc])
                            mi = t.matmul(bank(b), fslice(kc, oc),
                                     R1[:, kc * 512:(kc + 1) * 512], start=(kc == 0), stop=(kc == 7))
                        mi.then_inc(sT.h, sT.bump(1))
                    block.tensor(fmm)
                    FMM = sT.v

                    def fev(v, oc=oc, b=b, tb=tb, tgt=FMM):
                        v.wait_ge(sT.h, tgt)
                        xsl = X[:, oc * S + tb * 512:oc * S + tb * 512 + 512]
                        v.tensor_add(xsl, xsl, bank(b)).then_inc(sV.h, sV.bump(1))
                    block.vector(fev)
                    fev_hist = [fev_hist[1], sV.v]
                    fevs.append(sV.v)
                    if kv_live and oc >= 1:
                        wt3 = ln3_chunks[oc - 2] if oc >= 2 else last_stats_mm[0]
                        ln3_chunks.append(ln_chunk(oc - 1, tb * 512, True, wt3, sqw_v=fevs[oc - 1]))
                FFN_T = sT.v

                if not kv_live:
                    if tb + 1 < TB:
                        ln2_apps = [emit_ln((tb + 1) * 512, ydst, p3=True)] * 8
                    continue
                ln3_chunks.append(ln_chunk(7, tb * 512, True, ln3_chunks[-1], sqw_v=fevs[7]))
                last_stats_mm[0] = ln3_chunks[-1]
                ln3_apps = ln_finish(tb * 512, ydst, ln3_chunks[-1], extra_wait_t=FFN_T, p3=True)
                ln3_done = ln3_apps[-1]
                ln2_chunks = []
                for oc in range(8):
                    b = 4 + (oc % 2)

                    def kmm(t, oc=oc, b=b, apps=tuple(ln3_apps), lk=krot_hist[0]):
                        if lk:
                            t.wait_ge(sV.h, lk)
                        for kc in range(8):
                            if oc == 0:
                                t.wait_ge(sV.h, apps[kc])
                            mi = t.matmul(bank(b), E[:, kc * 2048 + oc * 128:kc * 2048 + oc * 128 + 128],
                                     R1[:, kc * 512:(kc + 1) * 512], start=(kc == 0), stop=(kc == 7))
                        mi.then_inc(sT.h, sT.bump(1))
                    block.tensor(kmm)
                    KMM = sT.v
                    if tb == TB - 1 and oc == 7:
                        def qw_load(g, kt=KMM, pw=sW.v):
                            g.wait_ge(sT.h, kt)
                            g.wait_ge(sW.h, pw)
                            g.dma_start(out=E[:, 0:16384].rearrange("p (c o) -> p c o", c=8)[:, :, 0:1024],
                                        in_=qw_h.rearrange("(c p) o -> p c o", p=128)).then_inc(sW.h, sW.bump(16))
                        block.gpsimd(qw_load)
                        QW_Q = sW.v

                    def kev(s, oc=oc, b=b, tgt=KMM, pk=krot_hist[1]):
                        s.wait_ge(sT.h, tgt)
                        if pk:
                            s.wait_ge(sV.h, pk)
                        s.activation(KRAW, bank(b), AF.Identity,
                                     bias=QKVB[:, 8 + oc:8 + oc + 1]).then_inc(sS.h, sS.bump(1))
                    block.scalar(kev)
                    KEV = sS.v

                    def ksw_g(y, tgt=KEV):
                        y.wait_ge(sS.h, tgt)
                        y.dma_start(out=KSW[0:32, :], in_=KRAW[32:64, :]).then_inc(sZS.h, sZS.bump(16))
                        y.dma_start(out=KSW[32:64, :], in_=KRAW[0:32, :]).then_inc(sZS.h, sZS.bump(16))
                        y.dma_start(out=KSW[64:96, :], in_=KRAW[96:128, :]).then_inc(sZS.h, sZS.bump(16))
                        y.dma_start(out=KSW[96:128, :], in_=KRAW[64:96, :]).then_inc(sZS.h, sZS.bump(16))
                    block.sync(ksw_g)
                    KSWD = sZS.v

                    def krot(v, oc=oc, tb=tb, tgt=KSWD, rz=sZ.v):
                        v.wait_ge(sZS.h, tgt)
                        v.wait_ge(sZ.h, rz)
                        tsl = slice(tb * 512, tb * 512 + 512)
                        v.tensor_mul(KRAW, KRAW, RC[:, tsl]).then_inc(sV.h, sV.bump(1))
                        v.wait_ge(sV.h, sV.v)
                        v.tensor_mul(KSW, KSW, RS[:, tsl]).then_inc(sV.h, sV.bump(1))
                        v.wait_ge(sV.h, sV.v)
                        v.tensor_add(KB[:, oc * S + tb * 512:oc * S + tb * 512 + 512], KRAW, KSW).then_inc(sV.h, sV.bump(1))
                    block.vector(krot)
                    krot_hist = [krot_hist[1], sV.v]
                KV_T = sT.v

                for u in range(4):
                    tc = tb * 4 + u
                    for oh in range(2):
                        bvb = 6 + (unit_vb % 2)

                        def vmmb(t, u=u, oh=oh, bvb=bvb, vt=ln3_done, lv=vevb_hist[0], qb=QBVB_DONE):
                            t.wait_ge(sV.h, max(vt, lv, qb))
                            for c in range(8):
                                mi = t.matmul(bank(bvb), R1[:, c * 512 + u * 128:c * 512 + u * 128 + 128],
                                         E[:, c * 2048 + 1024 + oh * 512:c * 2048 + 1024 + oh * 512 + 512],
                                         start=(c == 0), stop=False)
                            mi = t.matmul(bank(bvb), ONESR[:], QBVB[:, oh * 512:oh * 512 + 512],
                                     start=False, stop=True)
                            mi.then_inc(sT.h, sT.bump(1))
                        block.tensor(vmmb)
                        VMMB = sT.v

                        def vevb(v, tc=tc, oh=oh, bvb=bvb, tgt=VMMB):
                            v.wait_ge(sT.h, tgt)
                            dst = VT[:, tc * 1040 + oh * 520:tc * 1040 + oh * 520 + 520]
                            dst = dst.rearrange("p (h c) -> p h c", c=65)[:, :, 0:64]
                            v.tensor_copy(dst, bank(bvb)).then_inc(sV.h, sV.bump(1))
                        block.vector(vevb)
                        vevb_hist = [vevb_hist[1], sV.v]
                        unit_vb += 1
                        if tb + 1 < TB and len(ln2_chunks) < 8:
                            oc2 = len(ln2_chunks)
                            wt2 = ln2_chunks[oc2 - 1] if oc2 >= 1 else last_stats_mm[0]
                            ln2_chunks.append(ln_chunk(oc2, (tb + 1) * 512, True, wt2))

                if tb + 1 < TB:
                    last_stats_mm[0] = ln2_chunks[-1]
                    ln2_apps = ln_finish((tb + 1) * 512, ydst, ln2_chunks[-1],
                                         extra_wait_t=sT.v, p3=True)
                else:
                    def pjw_load(g, vt=sT.v, pw=sW.v):
                        g.wait_ge(sT.h, vt)
                        g.wait_ge(sW.h, pw)
                        g.dma_start(out=E[:, 0:16384].rearrange("p (c o) -> p c o", c=8)[:, :, 1024:2048],
                                    in_=pjw_h.rearrange("(c p) o -> p c o", p=128)).then_inc(sW.h, sW.bump(16))
                    block.gpsimd(pjw_load)
                    QW_ALL = sW.v

            barrier()

        if n_stages >= 3:
            barrier()

            def tri_build(v):
                v.tensor_scalar(STAIR[:], STAIR[:], -0.5, None, ALU.is_ge).then_inc(sV.h, sV.bump(1))
                v.wait_ge(sV.h, sV.v)
            block.vector(tri_build)
            CT2 = F1[0:65, 0:1024]
            ZRIB = F1[0:1, 2048:2560].bitcast(BF16)
            MEANb = F1[:, 1024:1536]
            RSTDb = F1[:, 1536:2048]
            QRAWb = F1[:, 2048:2560]
            QSWb = F1[:, 2560:3072]
            XBFb = QR[:, 3072:3584]
            SQBb = QR[:, 3584:4096]

            def ln_p4_chunk(c, toff, wt, wz=0):
                xsl = X[:, c * S + toff:c * S + toff + 512]

                def cst(v, xsl=xsl, wt=wt, c=c, wz=wz):
                    if wt:
                        v.wait_ge(sT.h, wt)
                    if c == 0 and wz:
                        v.wait_ge(sZF.h, wz)
                    v.tensor_copy(XBFb, xsl).then_inc(sV.h, sV.bump(1))
                    v.wait_ge(sV.h, sV.v)
                    v.tensor_mul(SQBb, XBFb, XBFb).then_inc(sV.h, sV.bump(1))
                block.vector(cst)
                CD = sV.v

                def smm(t, c=c, vt=CD):
                    t.wait_ge(sV.h, vt)
                    mi = t.matmul(bank(4), ONESQ[:], XBFb, start=(c == 0), stop=(c == 7))
                    mi = t.matmul(bank(5), ONESQ[:], SQBb, start=(c == 0), stop=(c == 7)).then_inc(sT.h, sT.bump(1))
                block.tensor(smm)
                return sT.v

            def emit_ln_p4(toff, ydst_fn, first_wait_t=0, wait_zf=0):
                smm_list = []
                for c in range(8):
                    wt = smm_list[c - 1] if c >= 1 else first_wait_t
                    smm_list.append(ln_p4_chunk(c, toff, wt, wait_zf))
                SMM_ALL = smm_list[-1]

                return ln_p4_finish(toff, ydst_fn, SMM_ALL)

            def ln_p4_finish(toff, ydst_fn, SMM_ALL, ew_t=0, ew_zf=0):
                def stv(v, tgt=SMM_ALL):
                    v.wait_ge(sT.h, tgt)
                    v.tensor_scalar_mul(MEANb, bank(4), 1.0 / 1024.0).then_inc(sV.h, sV.bump(1))
                    v.wait_ge(sV.h, sV.v)
                    v.tensor_mul(RSTDb, MEANb, MEANb).then_inc(sV.h, sV.bump(1))
                    v.wait_ge(sV.h, sV.v)
                    v.scalar_tensor_tensor(RSTDb, bank(5), 1.0 / 1024.0, RSTDb,
                                           ALU.mult, ALU.subtract).then_inc(sV.h, sV.bump(1))
                    v.wait_ge(sV.h, sV.v)
                    v.tensor_scalar_add(RSTDb, RSTDb, 1e-5).then_inc(sV.h, sV.bump(1))
                    v.wait_ge(sV.h, sV.v)
                block.vector(stv)
                VARD = sV.v

                def sts(s, tgt=VARD):
                    s.wait_ge(sV.h, tgt)
                    s.activation(RSTDb, RSTDb, AF.Ln).then_inc(sS.h, sS.bump(1))
                    s.wait_ge(sS.h, sS.v)
                    s.activation(RSTDb, RSTDb, AF.Exp, scale=-0.5).then_inc(sS.h, sS.bump(1))
                block.scalar(sts)
                RSD = sS.v

                appv = []
                for c in range(8):
                    xsl = X[:, c * S + toff:c * S + toff + 512]

                    def app(v, xsl=xsl, c=c, dst=ydst_fn(c), ew_t=ew_t, ew_zf=ew_zf):
                        if c == 0:
                            v.wait_ge(sS.h, RSD)
                            if ew_t:
                                t_ = ew_t
                                v.wait_ge(sT.h, t_)
                            if ew_zf:
                                v.wait_ge(sZF.h, ew_zf)
                        v.scalar_tensor_tensor(QRAWb, MEANb, -1.0, xsl, ALU.mult, ALU.add).then_inc(sV.h, sV.bump(1))
                        v.wait_ge(sV.h, sV.v)
                        v.tensor_mul(dst, QRAWb, RSTDb).then_inc(sV.h, sV.bump(1))
                        v.wait_ge(sV.h, sV.v)
                    block.vector(app)
                    appv.append(sV.v)
                return appv

            prev_exp_b = 0
            prev_evb = 0
            prev_ctxb = 0
            prev_zbf = 0
            prev_shift_z = 0
            prev_nrb_v = 0
            nrb_slot = {}
            shf_slot = {}
            pending_tail_b = [None]
            NQC = S // 512
            for qc in range(NQC):
                p4_apps = emit_ln_p4(qc * 512, ydst, first_wait_t=prev_zbf, wait_zf=prev_shift_z)

                qev_hist = [0, 0]
                qrot_chunk = {}
                QRAWq = [QRAWb, CX[:, 0:1024].bitcast(F32)]
                QSWq = [QSWb, CX[:, 1024:2048].bitcast(F32)]
                for oc in range(8):
                    b = 6 + (oc % 2)
                    QRW = QRAWq[oc % 2]
                    QSW_ = QSWq[oc % 2]

                    def qmm(t, oc=oc, b=b, apps=tuple(p4_apps), lq=qev_hist[0]):
                        if lq:
                            t.wait_ge(sV.h, lq)
                        if oc == 0:
                            t.wait_ge(sW.h, QW_Q)
                        for kc in range(8):
                            if oc == 0:
                                t.wait_ge(sV.h, apps[kc])
                            mi = t.matmul(bank(b), E[:, kc * 2048 + oc * 128:kc * 2048 + oc * 128 + 128],
                                     R1[:, kc * 512:(kc + 1) * 512], start=(kc == 0), stop=(kc == 7))
                        mi.then_inc(sT.h, sT.bump(1))
                    block.tensor(qmm)
                    QMM = sT.v
                    LAST_QMM = QMM

                    def qev(s, oc=oc, b=b, tgt=QMM, QRW=QRW, pq=qev_hist[1]):
                        s.wait_ge(sT.h, tgt)
                        if pq:
                            s.wait_ge(sV.h, pq)
                        s.activation(QRW, bank(b), AF.Identity,
                                     bias=QKVB[:, oc:oc + 1]).then_inc(sS.h, sS.bump(1))
                    block.scalar(qev)
                    QEV = sS.v

                    def qsw_g(y, tgt=QEV, QRW=QRW, QSW_=QSW_):
                        y.wait_ge(sS.h, tgt)
                        y.dma_start(out=QSW_[0:32, :], in_=QRW[32:64, :]).then_inc(sZS.h, sZS.bump(16))
                        y.dma_start(out=QSW_[32:64, :], in_=QRW[0:32, :]).then_inc(sZS.h, sZS.bump(16))
                        y.dma_start(out=QSW_[64:96, :], in_=QRW[96:128, :]).then_inc(sZS.h, sZS.bump(16))
                        y.dma_start(out=QSW_[96:128, :], in_=QRW[64:96, :]).then_inc(sZS.h, sZS.bump(16))
                    block.sync(qsw_g)
                    QSWD = sZS.v

                    def qrot(v, oc=oc, qc=qc, tgt=QSWD, rz=sZ.v, QRW=QRW, QSW_=QSW_):
                        v.wait_ge(sZS.h, tgt)
                        v.wait_ge(sZ.h, rz)
                        tsl = slice(qc * 512, qc * 512 + 512)
                        v.tensor_mul(QRW, QRW, RC[:, tsl]).then_inc(sV.h, sV.bump(1))
                        v.wait_ge(sV.h, sV.v)
                        v.tensor_mul(QSW_, QSW_, RS[:, tsl]).then_inc(sV.h, sV.bump(1))
                        v.wait_ge(sV.h, sV.v)
                        v.tensor_add(QR[:, oc * 512:oc * 512 + 512], QRW, QSW_).then_inc(sV.h, sV.bump(1))
                    block.vector(qrot)
                    qev_hist = [qev_hist[1], sV.v]
                    qrot_chunk[oc] = sV.v
                QROT_DONE = sV.v
                if qc == NQC - 1 and n_stages >= 4:
                    def ff2_load(g, kt=LAST_QMM, pw=sW.v):
                        g.wait_ge(sT.h, kt)
                        g.wait_ge(sW.h, pw)
                        g.dma_start(out=E[:, 0:16384].rearrange("p (c o) -> p c o", c=8)[:, :, 0:1024],
                                    in_=ffw_h.rearrange("(c p) o -> p c o", p=128)).then_inc(sW.h, sW.bump(16))
                    block.gpsimd(ff2_load)
                    FF2_W = sW.v
                    sign2_vals = []

                exp_done_qc = {}
                ctx_done_qc = {}
                for gidx in range(8):
                    h0, h1 = 2 * gidx, 2 * gidx + 1
                    ntc = 4 * qc + 4

                    def emit_scores_b(tc, hi_, h):
                        u = tc - 4 * qc
                        wv = exp_done_qc.get((gidx, tc - 1, hi_), prev_exp_b)
                        r0 = 64 * (h % 2)

                        def sc(t, tc=tc, hi_=hi_, h=h, u=u, wv=wv, r0=r0, qr=qrot_chunk[h // 2]):
                            if wv:
                                t.wait_ge(sS.h, wv)
                            if tc == 0:
                                t.wait_ge(sV.h, qr)
                            co = 128 * max(0, u)
                            lhsT = KB[r0:r0 + 64, (h // 2) * S + tc * 128:(h // 2) * S + tc * 128 + 128]
                            rhs = QR[r0:r0 + 64, (h // 2) * 512 + co:(h // 2) * 512 + 512]
                            mi = t.matmul(PS[:, hi_ * 512 + co:hi_ * 512 + 512], lhsT, rhs,
                                          start=True, stop=True,
                                          tile_position=(r0, 0))
                            mi.then_inc(sT.h, sT.bump(1))
                        block.tensor(sc)
                        return sT.v

                    def emit_exp_b(tc):
                        buf = tc % 2
                        sct = scb_done[(tc, 1)]
                        bufree = max(ctx_done_qc.get((gidx, tc - 2, 0), prev_ctxb),
                                     ctx_done_qc.get((gidx, tc - 2, 1), prev_ctxb))

                        co = 128 * max(0, tc - 4 * qc)

                        def ex(s, tc=tc, buf=buf, sct=sct, bufree=bufree, co=co):
                            s.wait_ge(sT.h, max(sct, bufree))
                            if co == 0:
                                s.activation(R1[:, buf * 1024:buf * 1024 + 1024],
                                             PS[:, 0:1024],
                                             AF.Exp, scale=0.125).then_inc(sS.h, sS.bump(1))
                            else:
                                s.activation(R1[:, buf * 1024 + co:buf * 1024 + 512],
                                             PS[:, co:512],
                                             AF.Exp, scale=0.125).then_inc(sS.h, sS.bump(1))
                                s.activation(R1[:, buf * 1024 + 512 + co:buf * 1024 + 1024],
                                             PS[:, 512 + co:1024],
                                             AF.Exp, scale=0.125).then_inc(sS.h, sS.bump(1))
                        block.scalar(ex)
                        exp_done_qc[(gidx, tc, 0)] = sS.v
                        exp_done_qc[(gidx, tc, 1)] = sS.v
                        EXPV = sS.v
                        u = tc - 4 * qc
                        out = {0: [("s", EXPV)], 1: [("s", EXPV)]}
                        if u >= 0:
                            for hi_ in (0, 1):
                                def tmsk(v, buf=buf, hi_=hi_, u=u, tgt=EXPV):
                                    v.wait_ge(sS.h, tgt)
                                    sl = R1[:, buf * 1024 + hi_ * 512 + 128 * u:buf * 1024 + hi_ * 512 + 128 * u + 128]
                                    v.tensor_mul(sl, sl, STAIR[:]).then_inc(sV.h, sV.bump(1))
                                block.vector(tmsk)
                                out[hi_].append(("v", sV.v))
                        return out

                    def emit_ctx_b(tc, hi_, h, rdy):
                        buf = tc % 2

                        def cx(t, tc=tc, hi_=hi_, h=h, buf=buf, rdy=tuple(rdy),
                               pe=prev_evb, ntc=ntc):
                            for kind, val in rdy:
                                if kind == "g":
                                    t.wait_ge(sG.h, val)
                                elif kind == "v":
                                    t.wait_ge(sV.h, val)
                                else:
                                    t.wait_ge(sS.h, val)
                            if tc == 0 and pe:
                                t.wait_ge(sV.h, pe)
                            lhsT = VT[:, tc * 1040 + 65 * h:tc * 1040 + 65 * h + 65]
                            co = 128 * max(0, tc - 4 * qc)
                            mi = t.matmul(PS[0:65, 1024 + hi_ * 512 + co:1024 + hi_ * 512 + 512],
                                          lhsT,
                                          R1[:, buf * 1024 + hi_ * 512 + co:buf * 1024 + hi_ * 512 + 512],
                                          start=(tc == 0), stop=(tc == ntc - 1))
                            mi.then_inc(sT.h, sT.bump(1))
                        block.tensor(cx)
                        ctx_done_qc[(gidx, tc, hi_)] = sT.v

                    scb_done = {}
                    rdy_map = {}
                    for tc in range(ntc):
                        for hi_, h in ((0, h0), (1, h1)):
                            scb_done[(tc, hi_)] = emit_scores_b(tc, hi_, h)
                        out = emit_exp_b(tc)
                        rdy_map[(tc, 0)] = out[0]
                        rdy_map[(tc, 1)] = out[1]
                        if tc > 0:
                            for hi_, h in ((0, h0), (1, h1)):
                                emit_ctx_b(tc - 1, hi_, h, rdy_map[(tc - 1, hi_)])
                    for hi_, h in ((0, h0), (1, h1)):
                        emit_ctx_b(ntc - 1, hi_, h, rdy_map[(ntc - 1, hi_)])
                    prev_exp_b = exp_done_qc[(gidx, ntc - 1, 1)]
                    CTXB = sT.v
                    prev_ctxb = CTXB

                    if pending_tail_b[0] is not None:
                        pending_tail_b[0]()
                        pending_tail_b[0] = None

                    if qc == NQC - 1 and n_stages >= 4 and gidx >= 2:
                        for _ in range(2):
                            if len(sign2_vals) < 8:
                                sc_ = len(sign2_vals)

                                def ff2_sign(s, sc_=sc_, tgt=FF2_W):
                                    s.wait_ge(sW.h, tgt)
                                    sl = E[:, sc_ * 2048:sc_ * 2048 + 1024]
                                    s.activation(sl, sl, AF.Sign).then_inc(sS.h, sS.bump(1))
                                block.scalar(ff2_sign)
                                sign2_vals.append(sS.v)

                    def evb(v, tgt=CTXB, pvv=sV.v):
                        v.wait_ge(sT.h, tgt)
                        v.wait_ge(sV.h, pvv)
                        v.tensor_copy(CT2[:, 0:1024], PS[0:65, 1024:2048]).then_inc(sV.h, sV.bump(1))
                    block.vector(evb)
                    EVB = sV.v
                    prev_evb = EVB

                    def zmb(y, tgt=EVB):
                        y.wait_ge(sV.h, tgt)
                        y.dma_start(out=ZM[:, 0:8], in_=F1[64:65, 0:1024]).then_inc(sZZ.h, sZZ.bump(16))
                    block.sync(zmb)
                    ZMB = sZZ.v

                    def rcb(v, tgt=ZMB):
                        v.wait_ge(sZZ.h, tgt)
                        v.reciprocal(ZMI[:, 0:8], ZM[:, 0:8]).then_inc(sV.h, sV.bump(1))
                    block.vector(rcb)
                    RCB = sV.v

                    def zrb(g, tgt=RCB, pz=prev_zbf):
                        g.wait_ge(sV.h, tgt)
                        if pz:
                            g.wait_ge(sT.h, pz)
                        g.dma_start(out=ZRIB, in_=ZMI[:, 0:8]).then_inc(sZI.h, sZI.bump(16))
                    block.gpsimd(zrb)
                    ZRB = sZI.v

                    def make_tail_b(ZRB=ZRB, gidx=gidx):
                        def emit_tail_b():
                            nonlocal prev_zbf, prev_nrb_v, prev_shift_z
                            for hi_ in range(2):
                                zb_b = 4 + hi_

                                def zbf(t, hi_=hi_, zb_b=zb_b, tgt=ZRB):
                                    t.wait_ge(sZI.h, tgt)
                                    mi = t.matmul(bank(zb_b, rows=64), ONESR[0:1, 0:64],
                                                  ZRIB[0:1, hi_ * 512:hi_ * 512 + 512], start=True, stop=True)
                                    mi.then_inc(sT.h, sT.bump(1))
                                block.tensor(zbf)
                                ZBF = sT.v
                                prev_zbf = ZBF

                                h_ = 2 * gidx + hi_
                                if h_ % 2 == 0:
                                    def nrb(v, hi_=hi_, h=h_, zb_b=zb_b, tgt=ZBF):
                                        v.wait_ge(sT.h, tgt)
                                        dst = CX[0:64, (h // 2) * 512:(h // 2) * 512 + 512]
                                        v.tensor_tensor(dst, bank(zb_b, rows=64),
                                                        CT2[0:64, hi_ * 512:hi_ * 512 + 512], ALU.mult).then_inc(sV.h, sV.bump(1))
                                    block.vector(nrb)
                                    prev_nrb_v = sV.v
                                    nrb_slot[h_ // 2] = sV.v
                                else:
                                    def nrb(v, hi_=hi_, zb_b=zb_b, tgt=ZBF, pshf=prev_shift_z):
                                        v.wait_ge(sT.h, tgt)
                                        if pshf:
                                            v.wait_ge(sZF.h, pshf)
                                        v.tensor_tensor(R1[0:64, 2048:2560], bank(zb_b, rows=64),
                                                        CT2[0:64, hi_ * 512:hi_ * 512 + 512], ALU.mult).then_inc(sV.h, sV.bump(1))
                                    block.vector(nrb)
                                    NRBv = sV.v
                                    prev_nrb_v = NRBv

                                    def shf(y3, h=h_, tgt=NRBv):
                                        y3.wait_ge(sV.h, tgt)
                                        y3.dma_start(out=CX[64:128, (h // 2) * 512:(h // 2) * 512 + 512],
                                                    in_=R1[0:64, 2048:2560]).then_inc(sZF.h, sZF.bump(16))
                                    block.sync(shf)
                                    prev_shift_z = sZF.v
                                    shf_slot[h_ // 2] = sZF.v
                        return emit_tail_b

                    pending_tail_b[0] = make_tail_b()

                for oc in range(2):
                    def pmm_p1(t, oc=oc, qc_=qc, pe=prev_exp_b,
                               ns=tuple(nrb_slot.get(k, 0) for k in range(7)),
                               ss=tuple(shf_slot.get(k, 0) for k in range(7))):
                        t.wait_ge(sS.h, pe)
                        if oc == 0 and qc_ == 0:
                            t.wait_ge(sW.h, QW_ALL)
                        for kc in range(7):
                            if oc == 0:
                                if ns[kc]:
                                    t.wait_ge(sV.h, ns[kc])
                                if ss[kc]:
                                    t.wait_ge(sZF.h, ss[kc])
                            mi = t.matmul(bank(oc), E[:, kc * 2048 + 1024 + oc * 128:kc * 2048 + 1024 + oc * 128 + 128],
                                     CX[:, kc * 512:(kc + 1) * 512], start=(kc == 0), stop=False)
                        mi.then_inc(sT.h, sT.bump(1))
                    block.tensor(pmm_p1)

                if pending_tail_b[0] is not None:
                    pending_tail_b[0]()
                    pending_tail_b[0] = None
                CXC_DONE = sV.v

                pev_hist = [0, 0]
                for oc in range(8):
                    b = oc % 2

                    if oc < 2:
                        def pmm(t, oc=oc, b=b, n7=nrb_slot[7], s7=shf_slot[7]):
                            t.wait_ge(sV.h, n7)
                            t.wait_ge(sZF.h, s7)
                            mi = t.matmul(bank(b), E[:, 7 * 2048 + 1024 + oc * 128:7 * 2048 + 1024 + oc * 128 + 128],
                                     CX[:, 7 * 512:8 * 512], start=False, stop=True)
                            mi.then_inc(sT.h, sT.bump(1))
                        block.tensor(pmm)
                    else:
                        def pmm(t, oc=oc, b=b, vt=CXC_DONE, lp=pev_hist[0], sz=prev_shift_z):
                            t.wait_ge(sV.h, max(vt, lp))
                            if sz:
                                t.wait_ge(sZF.h, sz)
                            for kc in range(8):
                                mi = t.matmul(bank(b), E[:, kc * 2048 + 1024 + oc * 128:kc * 2048 + 1024 + oc * 128 + 128],
                                         CX[:, kc * 512:(kc + 1) * 512], start=(kc == 0), stop=(kc == 7))
                            mi.then_inc(sT.h, sT.bump(1))
                        block.tensor(pmm)
                    PMM = sT.v

                    def pev(v, oc=oc, b=b, qc=qc, tgt=PMM):
                        v.wait_ge(sT.h, tgt)
                        xsl = X[:, oc * S + qc * 512:oc * S + qc * 512 + 512]
                        v.scalar_tensor_tensor(xsl, bank(b), PROJB[:, oc:oc + 1], xsl,
                                               ALU.add, ALU.add).then_inc(sV.h, sV.bump(1))
                    block.vector(pev)
                    pev_hist = [pev_hist[1], sV.v]

        if n_stages >= 4:
            barrier()

            fev2_hist = [0, 0]
            fev2_oc = {}
            for tb in range(TB):
                ln_done = emit_ln(tb * 512, ydst)
                for oc in range(8):
                    b = 2 + (oc % 2)

                    def fmm2(t, oc=oc, b=b, vt=ln_done, sgn=tuple(sign2_vals),
                             lf=fev2_hist[0], tb_=tb):
                        t.wait_ge(sV.h, max(vt, lf))
                        for kc in range(8):
                            if oc == 0 and tb_ == 0:
                                t.wait_ge(sS.h, sgn[kc])
                            mi = t.matmul(bank(b), E[:, kc * 2048 + oc * 128:kc * 2048 + oc * 128 + 128],
                                     R1[:, kc * 512:(kc + 1) * 512], start=(kc == 0), stop=(kc == 7))
                        mi.then_inc(sT.h, sT.bump(1))
                    block.tensor(fmm2)
                    FMM2 = sT.v

                    def fev2(v, oc=oc, b=b, tb=tb, tgt=FMM2):
                        v.wait_ge(sT.h, tgt)
                        xsl = X[:, oc * S + tb * 512:oc * S + tb * 512 + 512]
                        v.tensor_add(xsl, xsl, bank(b)).then_inc(sV.h, sV.bump(1))
                    block.vector(fev2)
                    fev2_hist = [fev2_hist[1], sV.v]
                    fev2_oc[oc] = sV.v
                if tb < TB - 1:
                    def out_tb(y, tb=tb, vt=sV.v):
                        y.wait_ge(sV.h, vt)
                        y.dma_start(out=out_h.rearrange("(c p) t -> p c t", p=128)[:, :, tb * 512:tb * 512 + 512],
                                    in_=X[:].rearrange("p (c t) -> p c t", c=8)[:, :, tb * 512:tb * 512 + 512]).then_inc(sO.h, sO.bump(16))
                    block.sync(out_tb)
                else:
                    for oc2 in range(8):
                        def out_oc(y, tb=tb, oc2=oc2, vt=fev2_oc[oc2]):
                            y.wait_ge(sV.h, vt)
                            y.dma_start(out=out_h.rearrange("(c p) t -> p c t", p=128)[:, oc2, tb * 512:tb * 512 + 512],
                                        in_=X[:, oc2 * S + tb * 512:oc2 * S + tb * 512 + 512]).then_inc(sO.h, sO.bump(16))
                        block.sync(out_oc)

        barrier()
        tfin, sfin, vfin, gfin = sT.v, sS.v, sV.v, sG.v

        def out_y(y, done=sO.v):
            if tfin: y.wait_ge(sT.h, tfin)
            if sfin: y.wait_ge(sS.h, sfin)
            if vfin: y.wait_ge(sV.h, vfin)
            if gfin: y.wait_ge(sG.h, gfin)
            if done == 0:
                y.dma_start(out=out_h.rearrange("(c p) t -> p c t", p=128), in_=X[:].rearrange("p (c t) -> p c t", c=8)).then_inc(sO.h, sO.bump(16))
            y.wait_ge(sO.h, sO.v)
        block.sync(out_y)

    es.close()
    return nc


def _prep_core_inputs(xb, Wq, bq, Wqc, bqc, Wk, bkv, Wkc, bkc, Wv, bv, Wvc, bvc,
                      qkv_w, qkv_b, proj_w, proj_b, ffn_w, S):
    f = np.float32
    d = {}
    d["x"] = np.ascontiguousarray(xb.T.astype(f))
    perm = np.zeros(256, np.int64)
    for h in range(16):
        g, p = h // 2, h % 2
        r = 64 * p + 32 * (g % 2) + 16 * ((g // 2) % 2)
        c = g // 4
        perm[c * 128 + r:c * 128 + r + 16] = np.arange(16 * h, 16 * h + 16)
    d["wa4"] = np.ascontiguousarray(np.concatenate(
        [Wq[:, perm], Wqc[:, perm], Wk, Wkc], axis=1).astype(f))
    d["wv2"] = np.ascontiguousarray(np.concatenate([Wv, Wvc], axis=1).astype(f))
    qkvT = np.ascontiguousarray(qkv_w.T.astype(f))
    d["kvw"] = np.ascontiguousarray(qkvT[:, 1024:3072])
    d["qw"] = np.ascontiguousarray(qkvT[:, 0:1024])
    d["pjw"] = np.ascontiguousarray(proj_w.T.astype(f))
    d["ffw"] = np.ascontiguousarray(ffn_w.T.astype(f))
    bqp = bq[perm]
    bqcp = bqc[perm]
    d["ba_qk"] = np.ascontiguousarray(
        np.stack([bqp[0:128], bqp[128:256],
                  bqcp[0:128], bqcp[128:256],
                  bkv.reshape(2, 128)[0], bkv.reshape(2, 128)[1],
                  bkc.reshape(2, 128)[0], bkc.reshape(2, 128)[1]], axis=1).astype(f))
    d["bv_row"] = np.ascontiguousarray(np.concatenate([bv, bvc])[None, :].astype(f))
    d["qkvb_qk"] = np.ascontiguousarray(
        np.concatenate([qkv_b[0:1024].reshape(8, 128).T,
                        qkv_b[1024:2048].reshape(8, 128).T], axis=1).astype(f))
    d["qkvb_v"] = np.ascontiguousarray(qkv_b[2048:3072][None, :].astype(f))
    d["projb"] = np.ascontiguousarray(proj_b.reshape(8, 128).T.astype(f))
    bk32 = np.zeros((128, 16), f)
    bkc32 = np.zeros((128, 16), f)
    for h in range(16):
        g, p = h // 2, h % 2
        r = 64 * p + 32 * (g % 2) + 16 * ((g // 2) % 2)
        bk32[r:r + 16, h] = bkv[16 * h:16 * h + 16]
        bkc32[r:r + 16, h] = bkc[16 * h:16 * h + 16]
    d["bk32"] = bk32
    d["bkc32"] = bkc32
    t64 = np.arange(S, dtype=np.float64)
    j = np.arange(64, dtype=np.float64) % 32
    invf = np.exp(-j * math.log(10000.0) / 32.0)
    theta = invf[:, None] * t64[None, :]
    d["tmat"] = np.broadcast_to(t64[None, :], (64, S)).astype(f).copy()
    d["jcol"] = j[:, None].astype(f)
    d["ksin"] = np.round(theta / (2 * math.pi)).astype(f)
    d["kcos"] = np.round((theta + math.pi / 2) / (2 * math.pi)).astype(f)
    st = np.where(np.arange(128)[None, :] < np.arange(128)[:, None],
                  np.float32(NEG), np.float32(0.0))
    d["stair"] = st.astype(ml_dtypes.bfloat16)
    d["ident"] = np.eye(128, dtype=f).astype(ml_dtypes.bfloat16)
    return d


def kernel(L0_attn_a_x, L0_attn_a_Wq, L0_attn_a_bq, L0_attn_a_Wqc, L0_attn_a_bqc,
           L0_attn_a_Wk, L0_attn_a_bk, L0_attn_a_Wkc, L0_attn_a_bkc,
           L0_attn_a_Wv, L0_attn_a_bv, L0_attn_a_Wvc, L0_attn_a_bvc,
           L0_attn_b_qkv_w, L0_attn_b_qkv_b, L0_attn_b_proj_w, L0_attn_b_proj_b,
           L0_ffn_weight):
    x = np.asarray(L0_attn_a_x, dtype=np.float32)
    B, S, _ = x.shape
    nc = build_nc(S=S)
    args = [np.asarray(a, dtype=np.float32) for a in
            (L0_attn_a_Wq, L0_attn_a_bq, L0_attn_a_Wqc, L0_attn_a_bqc,
             L0_attn_a_Wk, L0_attn_a_bk, L0_attn_a_Wkc, L0_attn_a_bkc,
             L0_attn_a_Wv, L0_attn_a_bv, L0_attn_a_Wvc, L0_attn_a_bvc,
             L0_attn_b_qkv_w, L0_attn_b_qkv_b, L0_attn_b_proj_w, L0_attn_b_proj_b,
             L0_ffn_weight)]
    in_maps = [_prep_core_inputs(x[b], *args, S) for b in range(B)]
    res = run_bass_kernel_spmd(nc, in_maps, core_ids=list(range(B)))
    out = np.stack([np.ascontiguousarray(np.asarray(r["out"]).T) for r in res.results], axis=0)
    return out.astype(np.float32)


# version 68
# speedup vs baseline: 1.0109x; 1.0109x over previous
import math
from contextlib import ExitStack

import numpy as np
import ml_dtypes

import concourse.bass as bass
import concourse.mybir as mybir
from concourse.bass_utils import run_bass_kernel_spmd

F32 = mybir.dt.float32
BF16 = mybir.dt.bfloat16
AF = mybir.ActivationFunctionType
ALU = mybir.AluOpType

D = 1024
NEG = -1.0e30
N_STAGES = 4


def _split_const(v):
    hi = np.float32(v)
    lo = np.float32(np.float64(v) - np.float64(hi))
    return float(hi), float(lo)


class Sem:
    def __init__(self, es, nc, name):
        self.h = es.enter_context(nc.semaphore(name))
        self.v = 0

    def bump(self, n):
        self.v += n
        return n


def build_nc(S=2048, n_stages=N_STAGES):
    nc = bass.Bass("TRN2", target_bir_lowering=False)
    TC = S // 128
    TB = S // 512
    QCW = min(1024, S)
    NQCA = S // QCW

    dr = {}

    def din(name, shape, dt=F32):
        dr[name] = nc.dram_tensor(name, list(shape), dt, kind="ExternalInput").ap()
        return dr[name]

    x_h = din("x", (D, S))
    wa4_h = din("wa4", (D, 1024))
    wv2_h = din("wv2", (D, 2048))
    kvw_h = din("kvw", (D, 2048))
    qw_h = din("qw", (D, 1024))
    pjw_h = din("pjw", (D, 1024))
    ffw_h = din("ffw", (D, 1024))
    din("ba_qk", (128, 8))
    din("bv_row", (1, 2048))
    din("qkvb_qk", (128, 16))
    din("qkvb_v", (1, 1024))
    din("projb", (128, 8))
    din("bk32", (128, 16))
    din("bkc32", (128, 16))
    din("tmat", (64, S))
    din("jcol", (64, 1))
    din("ksin", (64, S))
    din("kcos", (64, S))
    din("stair", (128, 128), BF16)
    din("ident", (128, 128), BF16)
    out_h = nc.dram_tensor("out", [D, S], F32, kind="ExternalOutput").ap()

    es = ExitStack()

    def sb(name, shape, dt):
        return es.enter_context(nc.sbuf_tensor(name, list(shape), dt))

    X = sb("X", (128, 8 * S), F32)
    E = sb("E", (128, 16384), BF16)
    KB = sb("KB", (128, 8 * S), BF16)
    VT = sb("VT", (128, TC * 1040), BF16)
    RC = sb("RC", (128, S), BF16)
    RS = sb("RS", (128, S), BF16)
    F1 = sb("F1", (128, 3072), F32)
    R1 = sb("R1", (128, 4096), BF16)
    QR = sb("QR", (128, 4096), BF16)
    CX = sb("CX", (128, 4096), BF16)
    STAIR = sb("STAIR", (128, 128), BF16)
    IDENT = sb("IDENT", (128, 128), BF16)
    ONESQ = sb("ONESQ", (128, 128), BF16)
    ONESR = sb("ONESR", (1, 128), BF16)
    BIASA = sb("BIASA", (128, 8), F32)
    QKVB = sb("QKVB", (128, 16), F32)
    PROJB = sb("PROJB", (128, 8), F32)
    ZM = sb("ZM", (128, 16), F32)
    ZMI = sb("ZMI", (128, 16), F32)
    QBVB = sb("QBVB", (1, 1024), BF16)
    ONE64 = sb("ONE64", (128, 1), BF16)
    LN16T = sb("LN16T", (128, 1), F32)

    BROWBF = QR[0:1, 0:2048]
    VPMA = QR[0:1, 2048:3088]
    KPMST = QR[:, 3088:3104]
    EPA = QR[0:1, 1024:1024 + QCW]
    VPMBD = CX[64:65, 0:520]
    EPB = CX[64:65, 1024:1024 + QCW]

    PS = es.enter_context(nc.psum_tensor("PS", [128, 4096], F32))

    def bank(i, rows=128):
        return PS[0:rows, 512 * i:512 * (i + 1)]

    class _PB:
        def __init__(self, nc):
            self.nc = nc

        def tensor(self, f):
            f(self.nc.tensor)

        def scalar(self, f):
            f(self.nc.scalar)

        def vector(self, f):
            f(self.nc.vector)

        def gpsimd(self, f):
            f(self.nc.gpsimd)

        def sync(self, f):
            f(self.nc.sync)

    if True:
        block = _PB(nc)
        sT = Sem(es, nc, "sT")
        sS = Sem(es, nc, "sS")
        sV = Sem(es, nc, "sV")
        sG = Sem(es, nc, "sG")
        sX = Sem(es, nc, "sX")
        sW = Sem(es, nc, "sW")
        sZ = Sem(es, nc, "sZ")
        sZK = Sem(es, nc, "sZK")
        sZS = Sem(es, nc, "sZS")
        sZZ = Sem(es, nc, "sZZ")
        sZF = Sem(es, nc, "sZF")
        sZI = Sem(es, nc, "sZI")
        sO = Sem(es, nc, "sO")
        sW2 = Sem(es, nc, "sW2")
        sX2 = Sem(es, nc, "sX2")
        sXB = Sem(es, nc, "sXB")
        sXC = Sem(es, nc, "sXC")
        sXD = Sem(es, nc, "sXD")

        def barrier():
            tv, sv, vv, gv = sT.v, sS.v, sV.v, sG.v

            def bt(t):
                if sv: t.wait_ge(sS.h, sv)
                if vv: t.wait_ge(sV.h, vv)
                if gv: t.wait_ge(sG.h, gv)
            block.tensor(bt)

            def bs(s):
                if tv: s.wait_ge(sT.h, tv)
                if vv: s.wait_ge(sV.h, vv)
                if gv: s.wait_ge(sG.h, gv)
            block.scalar(bs)

            def bv_(v):
                if tv: v.wait_ge(sT.h, tv)
                if sv: v.wait_ge(sS.h, sv)
                if gv: v.wait_ge(sG.h, gv)
            block.vector(bv_)

            def bg(g):
                if tv: g.wait_ge(sT.h, tv)
                if sv: g.wait_ge(sS.h, sv)
                if vv: g.wait_ge(sV.h, vv)
            block.gpsimd(bg)

        x_tb = {}

        def p0_loads(y):
            for name, dst in [("ba_qk", BIASA), ("qkvb_qk", QKVB), ("projb", PROJB),
                              ("bk32", CX[:, 0:32].bitcast(F32)),
                              ("bkc32", CX[:, 32:64].bitcast(F32)),
                              ("stair", STAIR), ("ident", IDENT)]:
                y.dma_start(out=dst[:], in_=dr[name]).then_inc(sX2.h, sX2.bump(16))
            kbf = KB[:].bitcast(F32)
            y.dma_start(out=kbf[0:64, 0:S], in_=dr["tmat"]).then_inc(sX2.h, sX2.bump(16))
            y.dma_start(out=kbf[0:64, S:2 * S], in_=dr["ksin"]).then_inc(sX2.h, sX2.bump(16))
            y.dma_start(out=kbf[0:64, 2 * S:3 * S], in_=dr["kcos"]).then_inc(sX2.h, sX2.bump(16))
            y.dma_start(out=ZM[0:64, 3:4], in_=dr["jcol"]).then_inc(sX2.h, sX2.bump(16))
            y.dma_start(out=F1[0:1, 0:2048], in_=dr["bv_row"]).then_inc(sX2.h, sX2.bump(16))
            y.dma_start(out=F1[0:1, 2048:3072], in_=dr["qkvb_v"]).then_inc(sX2.h, sX2.bump(16))
            for tb in range(TB):
                sl = slice(tb * 512, tb * 512 + 512)
                sem = (sX, sXB, sXC, sXD)[tb % 4]
                y.dma_start(out=X[:].rearrange("p (c t) -> p c t", c=8)[:, :, sl],
                            in_=x_h.rearrange("(c p) t -> p c t", p=128)[:, :, sl]).then_inc(sem.h, sem.bump(16))
                x_tb[tb] = (sem, sem.v)
        block.sync(p0_loads)
        X_LOADED = sX2.v

        wa4_kc = {}

        def p0_w(g):
            for kc in range(8):
                if kc:
                    g.wait_ge(sW.h, sW.v)
                g.dma_start(out=E[:, kc * 1024:(kc + 1) * 1024],
                            in_=wa4_h[kc * 128:(kc + 1) * 128, :]).then_inc(sW.h, sW.bump(16))
                wa4_kc[kc] = sW.v
        block.gpsimd(p0_w)
        WA4_LOADED = sW.v

        def p0_wv_pre(g, pw=WA4_LOADED):
            g.wait_ge(sW.h, pw)
            g.dma_start(out=E[:, 8192:16384].rearrange("p (c o) -> p c o", c=4),
                        in_=wv2_h[0:512, :].rearrange("(c p) o -> p c o", p=128)).then_inc(sW.h, sW.bump(16))
        block.gpsimd(p0_wv_pre)
        WV_PRE = sW.v

        def p0_const(v):
            v.memset(ONESQ[:], 1.0).then_inc(sV.h, sV.bump(1))
            v.wait_ge(sV.h, sV.v)
            v.memset(ONESR[:], 1.0).then_inc(sV.h, sV.bump(1))
            v.wait_ge(sV.h, sV.v)
            v.memset(ONE64[:], 1.0).then_inc(sV.h, sV.bump(1))
            v.wait_ge(sV.h, sV.v)
            v.memset(LN16T[:], math.log(16.0)).then_inc(sV.h, sV.bump(1))
            v.wait_ge(sV.h, sV.v)
            v.memset(VT[:, 64::65], 1.0).then_inc(sV.h, sV.bump(1))
        block.vector(p0_const)

        c_hi, c_lo = _split_const(math.log(10000.0) / 32.0)
        tp_hi, tp_lo = _split_const(2.0 * math.pi)
        hp_hi, hp_lo = _split_const(math.pi / 2.0)
        kbf = KB[:].bitcast(F32)
        TMAT = kbf[0:64, 0:S]
        KSIN = kbf[0:64, S:2 * S]
        KCOS = kbf[0:64, 2 * S:3 * S]
        JCOL = ZM[0:64, 3:4]
        TH = kbf[0:64, 3 * S:4 * S]
        U1 = kbf[0:64, 0:S]
        E1 = ZM[0:64, 0:1]
        EE = ZM[0:64, 1:2]
        FCOL = ZM[0:64, 2:3]

        def rope_v1(v):
            v.wait_ge(sX2.h, X_LOADED)
            v.tensor_scalar_mul(E1, JCOL, -c_hi).then_inc(sV.h, sV.bump(1))
            v.wait_ge(sV.h, sV.v)
            v.scalar_tensor_tensor(EE, JCOL, -c_lo, E1, ALU.mult, ALU.add).then_inc(sV.h, sV.bump(1))
        block.vector(rope_v1)
        ROPE_E = sV.v

        def rope_s1(s):
            s.wait_ge(sV.h, ROPE_E)
            s.activation(FCOL, EE, AF.Exp).then_inc(sS.h, sS.bump(1))
        block.scalar(rope_s1)
        ROPE_F = sS.v

        def rope_v2(v):
            v.wait_ge(sS.h, ROPE_F)
            v.tensor_scalar(TH, TMAT, FCOL, None, ALU.mult).then_inc(sV.h, sV.bump(1))
            v.wait_ge(sV.h, sV.v)
            v.scalar_tensor_tensor(U1, KSIN, -tp_hi, TH, ALU.mult, ALU.add).then_inc(sV.h, sV.bump(1))
            v.wait_ge(sV.h, sV.v)
            v.scalar_tensor_tensor(U1, KSIN, -tp_lo, U1, ALU.mult, ALU.add).then_inc(sV.h, sV.bump(1))
        block.vector(rope_v2)
        ROPE_U = sV.v

        def rope_s2(s):
            s.wait_ge(sV.h, ROPE_U)
            s.activation(RS[0:64, 0:S], U1, AF.Sin).then_inc(sS.h, sS.bump(1))
        block.scalar(rope_s2)
        ROPE_SIN = sS.v

        def rope_v3(v):
            v.wait_ge(sS.h, ROPE_SIN)
            v.tensor_scalar_mul(RS[0:32, 0:S], RS[0:32, 0:S], -1.0).then_inc(sV.h, sV.bump(1))
            v.wait_ge(sV.h, sV.v)
            v.scalar_tensor_tensor(U1, KCOS, -tp_hi, TH, ALU.mult, ALU.add).then_inc(sV.h, sV.bump(1))
            v.wait_ge(sV.h, sV.v)
            v.scalar_tensor_tensor(U1, KCOS, -tp_lo, U1, ALU.mult, ALU.add).then_inc(sV.h, sV.bump(1))
            v.wait_ge(sV.h, sV.v)
            v.tensor_scalar_add(U1, U1, hp_hi).then_inc(sV.h, sV.bump(1))
            v.wait_ge(sV.h, sV.v)
            v.tensor_scalar_add(U1, U1, hp_lo).then_inc(sV.h, sV.bump(1))
        block.vector(rope_v3)
        ROPE_UC = sV.v

        def rope_s3(s):
            s.wait_ge(sV.h, ROPE_UC)
            s.activation(RC[0:64, 0:S], U1, AF.Sin).then_inc(sS.h, sS.bump(1))
        block.scalar(rope_s3)
        ROPE_COS = sS.v

        def rope_g4(y):
            y.wait_ge(sS.h, ROPE_COS)
            y.dma_start(out=RC[64:128, 0:S], in_=RC[0:64, 0:S]).then_inc(sZ.h, sZ.bump(16))
            y.dma_start(out=RS[64:128, 0:S], in_=RS[0:64, 0:S]).then_inc(sZ.h, sZ.bump(16))
        block.sync(rope_g4)
        ROPE_DONE_Z = sZ.v

        def pad_s(s):
            s.wait_ge(sX2.h, X_LOADED)
            s.activation(ZMI[:, 0:16], CX[:, 32:64].bitcast(F32), AF.Sigmoid).then_inc(sS.h, sS.bump(1))
            s.wait_ge(sS.h, sS.v)
            s.activation(R1[0:1, 2048:4096].bitcast(F32), F1[0:1, 1024:2048], AF.Sigmoid).then_inc(sS.h, sS.bump(1))
        block.scalar(pad_s)
        PAD_SIG = sS.v

        def pad_v(v):
            v.wait_ge(sS.h, PAD_SIG)
            v.tensor_mul(KPMST[:], CX[:, 0:32].bitcast(F32), ZMI[:, 0:16]).then_inc(sV.h, sV.bump(1))
            v.tensor_mul(R1[0:1, 2048:4096].bitcast(F32), F1[0:1, 0:1024],
                         R1[0:1, 2048:4096].bitcast(F32)).then_inc(sV.h, sV.bump(1))
            v.wait_ge(sV.h, sV.v)
            stg = R1[0:1, 0:2080]
            v.memset(stg, 0.0).then_inc(sV.h, sV.bump(1))
            v.wait_ge(sV.h, sV.v)
            vp3 = R1[0:1, 2048:4096].bitcast(F32).rearrange("o (p two d) -> o p two d", two=2, d=64)
            v.tensor_copy(R1[0:1, 1040:1560].rearrange("o (p c) -> o p c", c=65)[:, :, 0:64],
                          vp3[:, :, 1, :]).then_inc(sV.h, sV.bump(1))
            v.wait_ge(sV.h, sV.v)
            v.memset(R1[0:1, 1040 + 64:1560:65], 1.0).then_inc(sV.h, sV.bump(1))
            v.wait_ge(sV.h, sV.v)
            v.tensor_scalar_mul(R1[0:1, 2048:4096].bitcast(F32), R1[0:1, 2048:4096].bitcast(F32),
                                16.0).then_inc(sV.h, sV.bump(1))
            v.wait_ge(sV.h, sV.v)
            v.tensor_copy(R1[0:1, 0:1040].rearrange("o (p c) -> o p c", c=130)[:, :, 0:64],
                          vp3[:, :, 0, :]).then_inc(sV.h, sV.bump(1))
            v.wait_ge(sV.h, sV.v)
            v.memset(R1[0:1, 64:1040:130], 16.0).then_inc(sV.h, sV.bump(1))
            v.wait_ge(sV.h, sV.v)
            v.tensor_copy(BROWBF, F1[0:1, 0:2048]).then_inc(sV.h, sV.bump(1))
            v.wait_ge(sV.h, sV.v)
            v.tensor_copy(QBVB[:], F1[0:1, 2048:3072]).then_inc(sV.h, sV.bump(1))
        block.vector(pad_v)
        PADC_V = sV.v
        QBVB_DONE = sV.v

        def vpm_g(y):
            y.wait_ge(sV.h, PADC_V)
            y.dma_start(out=VPMA, in_=R1[0:1, 0:1040]).then_inc(sZ.h, sZ.bump(16))
            y.dma_start(out=VPMBD, in_=R1[0:1, 1040:1560]).then_inc(sZ.h, sZ.bump(16))
        block.sync(vpm_g)
        PADC_DMA = sZ.v
        PADC_DONE = sV.v

        MEAN = F1[:, 2048:2560]
        RSTD = F1[:, 2560:3072]
        T1 = F1[:, 1024:1536]
        T1B = F1[:, 1536:2048]
        XBF = [F1[:, 0:256].bitcast(BF16), F1[:, 512:768].bitcast(BF16)]
        SQB = [F1[:, 256:512].bitcast(BF16), F1[:, 768:1024].bitcast(BF16)]
        XBF1 = [F1[:, 0:256].bitcast(BF16)] * 2
        SQB1 = [F1[:, 256:512].bitcast(BF16)] * 2
        last_stats_mm = [0]

        def ln_chunk(c, toff, p3, wait_t, sqw_v=0, sq_on_v=False):
            xbf_l = XBF1 if p3 else XBF
            sqb_l = SQB1 if p3 else SQB
            xb, sq = xbf_l[c % 2], sqb_l[c % 2]
            xsl = X[:, c * S + toff:c * S + toff + 512]

            def cast_fn(v, xb=xb, sq=sq, xsl=xsl, wait_t=wait_t, sqw_v=sqw_v):
                if wait_t:
                    v.wait_ge(sT.h, wait_t)
                if sqw_v:
                    v.wait_ge(sV.h, sqw_v)
                v.tensor_copy(xb, xsl).then_inc(sV.h, sV.bump(1))
                if sq_on_v:
                    v.wait_ge(sV.h, sV.v)
                    v.tensor_mul(sq, xb, xb).then_inc(sV.h, sV.bump(1))
            block.vector(cast_fn)
            vtgt = sV.v

            if not sq_on_v:
                def sq_fn(s, sq=sq, xsl=xsl, wait_t=wait_t, sqw_v=sqw_v):
                    if wait_t:
                        s.wait_ge(sT.h, wait_t)
                    if sqw_v:
                        s.wait_ge(sV.h, sqw_v)
                    s.activation(sq, xsl, AF.Square).then_inc(sS.h, sS.bump(1))
                block.scalar(sq_fn)
            stgt = sS.v if not sq_on_v else 0

            def mm_fn(t, xb=xb, sq=sq, c=c, vtgt=vtgt, stgt=stgt):
                t.wait_ge(sV.h, vtgt)
                if stgt:
                    t.wait_ge(sS.h, stgt)
                mi = t.matmul(bank(0), ONESQ[:], xb, start=(c == 0), stop=(c == 7))
                mi = t.matmul(bank(1), ONESQ[:], sq, start=(c == 0), stop=(c == 7)).then_inc(sT.h, sT.bump(1))
            block.tensor(mm_fn)
            return sT.v

        def emit_ln(toff, ydst_fn, extra_wait_t=0, p3=False):
            back = 1 if p3 else 2
            t_done_mm = []
            for c in range(8):
                wait_t = (t_done_mm[c - back] if c >= back else max(last_stats_mm[0], extra_wait_t))
                t_done_mm.append(ln_chunk(c, toff, p3, wait_t))
            last_stats_mm[0] = t_done_mm[-1]
            return ln_finish(toff, ydst_fn, t_done_mm[-1], extra_wait_t, p3)[-1]

        def ln_finish(toff, ydst_fn, STATS_MM, extra_wait_t=0, p3=False):
            t1 = T1B if p3 else T1

            def stat_v(v, STATS_MM=STATS_MM):
                v.wait_ge(sT.h, STATS_MM)
                v.tensor_scalar_mul(MEAN, bank(0), 1.0 / 1024.0).then_inc(sV.h, sV.bump(1))
                v.wait_ge(sV.h, sV.v)
                v.tensor_mul(RSTD, MEAN, MEAN).then_inc(sV.h, sV.bump(1))
                v.wait_ge(sV.h, sV.v)
                v.scalar_tensor_tensor(RSTD, bank(1), 1.0 / 1024.0, RSTD,
                                       ALU.mult, ALU.subtract).then_inc(sV.h, sV.bump(1))
                v.wait_ge(sV.h, sV.v)
                v.tensor_scalar_add(RSTD, RSTD, 1e-5).then_inc(sV.h, sV.bump(1))
                v.wait_ge(sV.h, sV.v)
            block.vector(stat_v)
            VAR_DONE = sV.v

            def stat_s(s):
                s.wait_ge(sV.h, VAR_DONE)
                s.activation(RSTD, RSTD, AF.Ln).then_inc(sS.h, sS.bump(1))
                s.wait_ge(sS.h, sS.v)
                s.activation(RSTD, RSTD, AF.Exp, scale=-0.5).then_inc(sS.h, sS.bump(1))
            block.scalar(stat_s)
            RSTD_DONE = sS.v

            app_vals = []
            for c in range(8):
                xsl = X[:, c * S + toff:c * S + toff + 512]

                def app_v(v, xsl=xsl, c=c, dst=ydst_fn(c), ew=extra_wait_t, t1=t1):
                    if c == 0:
                        v.wait_ge(sS.h, RSTD_DONE)
                        if ew:
                            v.wait_ge(sT.h, ew)
                    v.scalar_tensor_tensor(t1, MEAN, -1.0, xsl, ALU.mult, ALU.add).then_inc(sV.h, sV.bump(1))
                    v.wait_ge(sV.h, sV.v)
                    v.tensor_mul(dst, t1, RSTD).then_inc(sV.h, sV.bump(1))
                    v.wait_ge(sV.h, sV.v)
                block.vector(app_v)
                app_vals.append(sV.v)
            return app_vals

        def ydst(c):
            return R1[:, c * 512:(c + 1) * 512]

        QA_OFF = 4 * S
        KD_OFF = 6 * S

        if n_stages >= 1:
            def p1a_gate(t):
                t.wait_ge(x_tb[0][0].h, x_tb[0][1])
            block.tensor(p1a_gate)

            def p1a_gate_v(v):
                v.wait_ge(sZ.h, PADC_DMA)
                v.wait_ge(x_tb[0][0].h, x_tb[0][1])
            block.vector(p1a_gate_v)

            def p1a_gate_s(s):
                s.wait_ge(sV.h, PADC_V)
                s.wait_ge(x_tb[0][0].h, x_tb[0][1])
            block.scalar(p1a_gate_s)

            ev_hist = [0, 0]
            sig_hist = [0, 0]
            unit_i = 0
            NCACHE = min(TB, (4 * S) // 4096)
            cache_first = TB - NCACHE
            p1_apps = {}

            def yslice(tb, lo, hi):
                if tb >= cache_first:
                    base = (tb - cache_first) * 4096
                    return KB[:, base + lo:base + hi]
                return R1[:, lo:hi]

            for tb in range(TB):
                if tb > 0:
                    def xg_v(v, xs=x_tb[tb][0], xv=x_tb[tb][1]):
                        v.wait_ge(xs.h, xv)
                    block.vector(xg_v)

                    def xg_s(s, xs=x_tb[tb][0], xv=x_tb[tb][1]):
                        s.wait_ge(xs.h, xv)
                    block.scalar(xg_s)
                yd = (lambda c, tb=tb: yslice(tb, c * 512, (c + 1) * 512))
                ln_done = emit_ln(tb * 512, yd)
                p1_apps[tb] = ln_done
                for pi, (o_raw, o_gate, b_raw, b_gate, dst_off) in enumerate([
                        (0, 256, 0, 2, QA_OFF),
                        (512, 768, 4, 6, KD_OFF)]):
                    for oc in range(2):
                        br = 2 + 2 * (unit_i % 2)
                        bg = br + 1

                        def mm_fn(t, o_raw=o_raw, o_gate=o_gate, oc=oc, br=br, bg=bg, tb=tb,
                                  vt=ln_done, first=(tb == 0 and unit_i == 0),
                                  lev=ev_hist[0], lsig=sig_hist[0]):
                            t.wait_ge(sV.h, max(vt, lev))
                            if lsig:
                                t.wait_ge(sS.h, lsig)
                            for kc in range(8):
                                if first:
                                    t.wait_ge(sW.h, wa4_kc[kc])
                                mi = t.matmul(bank(br), E[:, kc * 1024 + o_raw + oc * 128:kc * 1024 + o_raw + oc * 128 + 128],
                                         yslice(tb, kc * 512, (kc + 1) * 512), start=(kc == 0), stop=(kc == 7))
                            for kc in range(8):
                                mi = t.matmul(bank(bg), E[:, kc * 1024 + o_gate + oc * 128:kc * 1024 + o_gate + oc * 128 + 128],
                                         yslice(tb, kc * 512, (kc + 1) * 512), start=(kc == 0), stop=(kc == 7))
                            mi.then_inc(sT.h, sT.bump(1))
                        block.tensor(mm_fn)
                        MM_DONE = sT.v
                        GT = F1[:, 1536:2048]
                        bgt = BIASA[:, b_gate + oc:b_gate + oc + 1]
                        braw = BIASA[:, b_raw + oc:b_raw + oc + 1]

                        def sig_fn(s, bgt=bgt, bg=bg, tgt=MM_DONE, lev=ev_hist[1]):
                            s.wait_ge(sT.h, tgt)
                            if lev:
                                s.wait_ge(sV.h, lev)
                            s.activation(GT, bank(bg), AF.Sigmoid, bias=bgt).then_inc(sS.h, sS.bump(1))
                        block.scalar(sig_fn)
                        sig_hist = [sig_hist[1], sS.v]

                        dst = KB[:, dst_off + oc * S + tb * 512: dst_off + oc * S + tb * 512 + 512]

                        def ev_fn(v, braw=braw, br=br, dst=dst, tgt=sS.v):
                            v.wait_ge(sS.h, tgt)
                            v.scalar_tensor_tensor(dst, bank(br), braw, GT, ALU.add, ALU.mult).then_inc(sV.h, sV.bump(1))
                        block.vector(ev_fn)
                        ev_hist = [ev_hist[1], sV.v]
                        unit_i += 1
            P1A_V = sV.v
            P1A_T = sT.v

            def wv_g(g, tt=P1A_T, pw=WV_PRE):
                g.wait_ge(sT.h, tt)
                g.wait_ge(sW.h, pw)
                g.dma_start(out=E[:, 0:8192].rearrange("p (c o) -> p c o", c=4),
                            in_=wv2_h[512:1024, :].rearrange("(c p) o -> p c o", p=128)).then_inc(sW.h, sW.bump(16))
            block.gpsimd(wv_g)
            WV_LOADED = sW.v

            def wsl(c, off):
                base = 8192 + c * 2048 if c < 4 else (c - 4) * 2048
                return E[:, base + off:base + off + 512]

            vev_hist = [0, 0]
            vsig_hist = [0, 0]
            unit_v = 0
            vev_tc = {}
            kred_ch = {}
            for tb in range(TB):
                if tb >= cache_first:
                    ln_done = p1_apps[tb]
                else:
                    ln_done = emit_ln(tb * 512, ydst)
                for u in range(4):
                    tc = tb * 4 + u
                    for oh in range(2):
                        br = 2 + 2 * (unit_v % 2)
                        bg = br + 1

                        first = (tb == 0 and u == 0 and oh == 0)

                        def vmm_fn(t, u=u, oh=oh, br=br, bg=bg, tb=tb, vt=ln_done,
                                   first=first, lv=vev_hist[0], ls=vsig_hist[0]):
                            t.wait_ge(sV.h, max(vt, lv))
                            if ls:
                                t.wait_ge(sS.h, ls)
                            for c in range(8):
                                if first:
                                    if c == 0:
                                        t.wait_ge(sW.h, WV_PRE)
                                    elif c == 4:
                                        t.wait_ge(sW.h, WV_LOADED)
                                mi = t.matmul(bank(br), yslice(tb, c * 512 + u * 128, c * 512 + u * 128 + 128),
                                         wsl(c, oh * 512),
                                         start=(c == 0), stop=False)
                            mi = t.matmul(bank(br), ONESR[:], BROWBF[:, oh * 512:oh * 512 + 512],
                                     start=False, stop=True)
                            for c in range(8):
                                mi = t.matmul(bank(bg), yslice(tb, c * 512 + u * 128, c * 512 + u * 128 + 128),
                                         wsl(c, 1024 + oh * 512),
                                         start=(c == 0), stop=False)
                            mi = t.matmul(bank(bg), ONESR[:], BROWBF[:, 1024 + oh * 512:1024 + oh * 512 + 512],
                                     start=False, stop=True)
                            mi.then_inc(sT.h, sT.bump(1))
                        block.tensor(vmm_fn)
                        VMM = sT.v
                        GT = F1[:, 1536:2048]

                        def vsig_fn(s, tgt=VMM, bg=bg, lv=vev_hist[1]):
                            s.wait_ge(sT.h, tgt)
                            if lv:
                                s.wait_ge(sV.h, lv)
                            s.activation(GT, bank(bg), AF.Sigmoid).then_inc(sS.h, sS.bump(1))
                        block.scalar(vsig_fn)
                        vsig_hist = [vsig_hist[1], sS.v]

                        def vev_fn(v, tc=tc, oh=oh, br=br, tgt=sS.v):
                            v.wait_ge(sS.h, tgt)
                            dst = VT[:, tc * 1040 + oh * 520:tc * 1040 + oh * 520 + 520]
                            dst = dst.rearrange("p (h c) -> p h c", c=65)[:, :, 0:64]
                            v.tensor_tensor(dst, bank(br), GT, ALU.mult).then_inc(sV.h, sV.bump(1))
                        block.vector(vev_fn)
                        vev_hist = [vev_hist[1], sV.v]
                        vev_tc[tc] = sV.v
                        unit_v += 1
                if NCACHE == 0 and tb == 0:
                    lohi = (0, 4 * S)
                elif tb >= cache_first:
                    lo = (tb - cache_first) * 4096
                    lohi = (lo, min(lo + 4096, 4 * S))
                else:
                    lohi = None
                if lohi is not None:
                    lo, hi = lohi

                    def kz_v(v, tt=sT.v, zd=PADC_DMA, lo=lo, hi=hi, first=(lo == 0)):
                        v.wait_ge(sT.h, tt)
                        if first:
                            v.wait_ge(sZ.h, zd)
                        v.memset(KB[:, lo:hi], 0.0).then_inc(sV.h, sV.bump(1))
                    block.vector(kz_v)
                    KZ_DONE = sV.v

                    def kred_pair(h):
                        src = KB[16 * (h % 8):16 * (h % 8) + 16,
                                 KD_OFF + (h // 8) * S:KD_OFF + (h // 8) * S + S]
                        g = h // 2
                        ch = g // 2
                        r0 = 64 * (h % 2) + 32 * (g % 2) + 16 * ((g // 2) % 2)
                        return KB[r0:r0 + 16, ch * S:(ch + 1) * S], src

                    heads_tb = [h for h in range(16) if lo <= ((h // 2) // 2) * S < hi]
                    ksem = (sX, sXB, sXC, sXD)[tb % 4]

                    def kred_y(y, kz=KZ_DONE, hs=tuple(heads_tb[0::2]), ksem=ksem):
                        y.wait_ge(sV.h, kz)
                        for h in hs:
                            dst, src = kred_pair(h)
                            y.dma_start(out=dst, in_=src).then_inc(ksem.h, ksem.bump(16))
                    block.sync(kred_y)

                    def kred_s(s, kz=KZ_DONE, hs=tuple(heads_tb[1::2]), ksem=ksem):
                        s.wait_ge(sV.h, kz)
                        for h in hs:
                            dst, src = kred_pair(h)
                            s.dma_start(out=dst, in_=src).then_inc(ksem.h, ksem.bump(16))
                    block.scalar(kred_s)
                    for ch in range(lo // S, max(lo // S + 1, hi // S)):
                        kred_ch[ch] = (ksem, ksem.v)
            KRED_ALL = [(sm, sm.v) for sm in (sX, sXB, sXC, sXD)]
            P1B_T = sT.v

        if n_stages >= 1:
            P2_V0 = sV.v
            P2_S0 = sS.v

            if n_stages >= 2:
                def kv_prefetch(g, tt=P1B_T, pw=sW.v):
                    g.wait_ge(sT.h, tt)
                    g.wait_ge(sW.h, pw)
                    g.dma_start(out=E[:, 0:16384].rearrange("p (c o) -> p c o", c=8),
                                in_=kvw_h.rearrange("(c p) o -> p c o", p=128)).then_inc(sW.h, sW.bump(16))
                block.gpsimd(kv_prefetch)

            CTa = F1[0:65, 0:QCW]
            CTb = F1[0:65, QCW:2 * QCW]
            if 2 * QCW <= 1024:
                ZRI = F1[0:1, 2048:2048 + QCW].bitcast(BF16)
            else:
                ZRI = KB[0:1, 6 * S:6 * S + 2 * QCW]
            NH = QCW // 512
            exp_done = {}
            ctx_done = {}
            pending_tail = [None]
            prev_nrm_v = P2_V0
            prev_exp_last = P2_S0
            prev_ctx_all = 0
            prev_zb_t = 0
            prev_xadd_g = 0
            P2_MARK = {}
            for qc in range(NQCA):
                for gidx in range(8):
                    if qc == NQCA - 1 and gidx == 1:
                        P2_MARK["g"] = sG.v
                        P2_MARK["zz"] = sZZ.v
                    h0, h1 = 2 * gidx, 2 * gidx + 1
                    qa_ch = gidx // 4

                    def pad_mm(t, gidx=gidx, qc=qc, qa_ch=qa_ch,
                               pe=prev_exp_last, pc=max(PADC_DONE, P2_V0), pd=PADC_DMA):
                        if pe:
                            t.wait_ge(sS.h, pe)
                        t.wait_ge(sV.h, pc)
                        t.wait_ge(sZ.h, pd)
                        for half in range(NH):
                            for p in range(2):
                                base = 64 * p + 32 * (gidx % 2)
                                rhs = KB[base:base + 32,
                                         QA_OFF + qa_ch * S + qc * QCW + half * 512:
                                         QA_OFF + qa_ch * S + qc * QCW + half * 512 + 512]
                                mi = t.matmul(PS[0:1, p * 1024 + half * 512:p * 1024 + half * 512 + 512],
                                              KPMST[base:base + 32, 2 * gidx + p:2 * gidx + p + 1], rhs,
                                              start=True, stop=True, tile_position=(base, 0))
                        mi.then_inc(sT.h, sT.bump(1))
                    block.tensor(pad_mm)
                    PADMM = sT.v

                    def pad_exp(s, tgt=PADMM, pca=prev_ctx_all):
                        s.wait_ge(sT.h, tgt)
                        if pca:
                            s.wait_ge(sT.h, pca)
                        s.activation(EPA, PS[0:1, 0:QCW], AF.Exp, scale=0.125).then_inc(sS.h, sS.bump(1))
                        s.activation(EPB, PS[0:1, 1024:1024 + QCW], AF.Exp,
                                     bias=LN16T[64:65, 0:1], scale=0.125).then_inc(sS.h, sS.bump(1))
                    block.scalar(pad_exp)
                    EP_DONE = sS.v

                    def emit_scores(tc, hi_, h):
                        ksl_ch = gidx // 2
                        base = 64 * hi_ + 32 * (gidx % 2)
                        wv = exp_done.get((gidx, tc - 1, hi_), EP_DONE)

                        def sc(t, tc=tc, hi_=hi_, base=base, ksl_ch=ksl_ch, wv=wv,
                               qa_ch=qa_ch, qc=qc, kr=(kred_ch[ksl_ch] if qc == 0 and tc == 0 else None),
                               v0=(P2_V0 if qc == 0 and tc == 0 else 0)):
                            t.wait_ge(sS.h, wv)
                            if kr is not None:
                                t.wait_ge(kr[0].h, kr[1])
                            if v0:
                                t.wait_ge(sV.h, v0)
                            lhsT = KB[base:base + 32, ksl_ch * S + tc * 128:ksl_ch * S + tc * 128 + 128]
                            for half in range(NH):
                                rhs = KB[base:base + 32,
                                         QA_OFF + qa_ch * S + qc * QCW + half * 512:
                                         QA_OFF + qa_ch * S + qc * QCW + half * 512 + 512]
                                mi = t.matmul(bank(hi_ * NH + half), lhsT, rhs,
                                              start=True, stop=True, tile_position=(base, 0))
                            mi.then_inc(sT.h, sT.bump(1))
                        block.tensor(sc)
                        return sT.v

                    def emit_exp(tc, hi_):
                        buf = tc % 2
                        sct = sc_done[(tc, hi_)]
                        bufree = ctx_done.get((gidx, tc - 2, hi_), prev_ctx_all)

                        def ex(s, tc=tc, hi_=hi_, buf=buf, sct=sct, bufree=bufree):
                            s.wait_ge(sT.h, max(sct, bufree))
                            s.activation(R1[:, buf * 2048 + hi_ * QCW:buf * 2048 + hi_ * QCW + QCW],
                                         PS[:, hi_ * QCW:hi_ * QCW + QCW],
                                         AF.Exp, scale=0.125).then_inc(sS.h, sS.bump(1))
                        block.scalar(ex)
                        exp_done[(gidx, tc, hi_)] = sS.v
                        return sS.v

                    def emit_ctx(tc, hi_, h):
                        buf = tc % 2
                        ev = exp_done[(gidx, tc, hi_)]
                        pw = prev_nrm_v if tc == 0 else 0

                        def cx(t, tc=tc, hi_=hi_, h=h, buf=buf, ev=ev, pw=pw,
                               vt=(vev_tc[tc] if qc == 0 else 0)):
                            t.wait_ge(sS.h, ev)
                            if pw:
                                t.wait_ge(sV.h, pw)
                            if vt:
                                t.wait_ge(sV.h, vt)
                            lhsT = VT[:, tc * 1040 + 65 * h:tc * 1040 + 65 * h + 65]
                            for half in range(NH):
                                mi = t.matmul(PS[0:65, 2048 + hi_ * QCW + half * 512:2048 + hi_ * QCW + half * 512 + 512],
                                              lhsT,
                                              R1[:, buf * 2048 + hi_ * QCW + half * 512:buf * 2048 + hi_ * QCW + half * 512 + 512],
                                              start=(tc == 0), stop=False)
                            mi.then_inc(sT.h, sT.bump(1))
                        block.tensor(cx)
                        ctx_done[(gidx, tc, hi_)] = sT.v

                    sc_done = {}
                    for tc in range(TC):
                        for hi_, h in ((0, h0), (1, h1)):
                            sc_done[(tc, hi_)] = emit_scores(tc, hi_, h)
                        for hi_ in (0, 1):
                            emit_exp(tc, hi_)
                        if tc == 0 and pending_tail[0] is not None:
                            pending_tail[0]()
                            pending_tail[0] = None
                        if tc > 0:
                            for hi_, h in ((0, h0), (1, h1)):
                                emit_ctx(tc - 1, hi_, h)
                    for hi_, h in ((0, h0), (1, h1)):
                        emit_ctx(TC - 1, hi_, h)
                    prev_exp_last = exp_done[(gidx, TC - 1, 1)]

                    def padctx_fn(t, gidx=gidx):
                        for half in range(NH):
                            mi = t.matmul(PS[0:65, 2048 + half * 512:2048 + half * 512 + 512],
                                          VPMA[:, 130 * gidx:130 * gidx + 65],
                                          EPA[:, half * 512:half * 512 + 512],
                                          start=False, stop=True)
                        for half in range(NH):
                            mi = t.matmul(PS[0:65, 2048 + QCW + half * 512:2048 + QCW + half * 512 + 512],
                                          VPMBD[:, 65 * gidx:65 * gidx + 65],
                                          EPB[:, half * 512:half * 512 + 512],
                                          start=False, stop=True)
                        mi.then_inc(sT.h, sT.bump(1))
                    block.tensor(padctx_fn)
                    prev_ctx_all = sT.v
                    CTX_ALL = sT.v

                    def evac_v(v, tgt=CTX_ALL, pvv=sV.v):
                        v.wait_ge(sT.h, tgt)
                        v.wait_ge(sV.h, pvv)
                        v.tensor_copy(F1[0:65, 0:2 * QCW], PS[0:65, 2048:2048 + 2 * QCW]).then_inc(sV.h, sV.bump(1))
                    block.vector(evac_v)
                    EVAC = sV.v

                    nzm = 2 * QCW // 128

                    def zm_g(y, tgt=EVAC):
                        y.wait_ge(sV.h, tgt)
                        y.dma_start(out=ZM[:, 0:nzm], in_=F1[64:65, 0:2 * QCW]).then_inc(sZZ.h, sZZ.bump(16))
                    block.sync(zm_g)
                    ZMD = sZZ.v

                    def recip_v(v, tgt=ZMD):
                        v.wait_ge(sZZ.h, tgt)
                        v.reciprocal(ZMI[:, 0:nzm], ZM[:, 0:nzm]).then_inc(sV.h, sV.bump(1))
                    block.vector(recip_v)
                    RECIP = sV.v

                    def zri_g(g, tgt=RECIP, pzb=prev_zb_t,
                              kr=(tuple(KRED_ALL) if qc == 0 and gidx == 0 else None)):
                        g.wait_ge(sV.h, tgt)
                        if kr is not None:
                            for sm, vv in kr:
                                if vv:
                                    g.wait_ge(sm.h, vv)
                        if pzb:
                            g.wait_ge(sT.h, pzb)
                        g.dma_start(out=ZRI, in_=ZMI[:, 0:nzm]).then_inc(sZI.h, sZI.bump(16))
                    block.gpsimd(zri_g)
                    ZRID = sZI.v

                    def make_tail(ZRID=ZRID, gidx=gidx, qc=qc):
                        def emit_tail():
                            nonlocal prev_nrm_v, prev_zb_t, prev_xadd_g
                            units = [(hi_, half) for hi_ in range(2) for half in range(NH)]
                            nrm_vals = []
                            for ui, (hi_, half) in enumerate(units):
                                zb_b = 4 + ui % 2
                                bank_free = nrm_vals[ui - 2] if ui >= 2 else 0

                                def zb_fn(t, hi_=hi_, half=half, zb_b=zb_b, tgt=ZRID, pn=bank_free):
                                    t.wait_ge(sZI.h, tgt)
                                    if pn:
                                        t.wait_ge(sV.h, pn)
                                    mi = t.matmul(bank(zb_b, rows=64), ONESR[0:1, 0:64],
                                                  ZRI[0:1, hi_ * QCW + half * 512:hi_ * QCW + half * 512 + 512],
                                                  start=True, stop=True)
                                    mi.then_inc(sT.h, sT.bump(1))
                                block.tensor(zb_fn)
                                ZB = sT.v
                                prev_zb_t = ZB

                                def nrm_fn(v, hi_=hi_, half=half, zb_b=zb_b, ui=ui, tgt=ZB, pg=prev_xadd_g):
                                    v.wait_ge(sT.h, tgt)
                                    if pg:
                                        v.wait_ge(sG.h, pg)
                                    ct = F1[0:64, hi_ * QCW + half * 512:hi_ * QCW + half * 512 + 512]
                                    v.tensor_tensor(CX[0:64, (ui % 4) * 512:(ui % 4) * 512 + 512],
                                                    bank(zb_b, rows=64), ct, ALU.mult).then_inc(sV.h, sV.bump(1))
                                block.vector(nrm_fn)
                                NRM = sV.v
                                prev_nrm_v = NRM
                                nrm_vals.append(NRM)

                                h_ = 2 * gidx + hi_
                                if h_ % 2 == 1:
                                    def shift_g(y2, ui=ui, tgt=NRM):
                                        y2.wait_ge(sV.h, tgt)
                                        y2.dma_start(out=CX[64:128, 2048 + (ui % 4) * 512:2048 + (ui % 4) * 512 + 512],
                                                     in_=CX[0:64, (ui % 4) * 512:(ui % 4) * 512 + 512]).then_inc(sZF.h, sZF.bump(16))
                                    block.sync(shift_g)
                                    SHF = sZF.v

                                    def xadd_fn(g2, half=half, gidx=gidx, qc=qc, ui=ui, h=h_, tgt=SHF):
                                        g2.wait_ge(sZF.h, tgt)
                                        xsl = X[64:128, (h // 2) * S + qc * QCW + half * 512:(h // 2) * S + qc * QCW + half * 512 + 512]
                                        cxs = CX[64:128, 2048 + (ui % 4) * 512:2048 + (ui % 4) * 512 + 512]
                                        g2.tensor_add(xsl, xsl, cxs).then_inc(sG.h, sG.bump(1))
                                    block.gpsimd(xadd_fn)
                                else:
                                    def xadd_fn(g2, half=half, gidx=gidx, qc=qc, ui=ui, h=h_, tgt=NRM):
                                        g2.wait_ge(sV.h, tgt)
                                        xsl = X[0:64, (h // 2) * S + qc * QCW + half * 512:(h // 2) * S + qc * QCW + half * 512 + 512]
                                        cxs = CX[0:64, (ui % 4) * 512:(ui % 4) * 512 + 512]
                                        g2.tensor_add(xsl, xsl, cxs).then_inc(sG.h, sG.bump(1))
                                    block.gpsimd(xadd_fn)
                                prev_xadd_g = sG.v
                        return emit_tail

                    pending_tail[0] = make_tail()
            P2_QR_FREE = sT.v
            if pending_tail[0] is not None:
                pending_tail[0]()
                pending_tail[0] = None

            if n_stages >= 2:
                LAST_EXP = sS.v
                LAST_CTX = prev_ctx_all

                def ln20_gate_v(v, gg=(P2_MARK["g"] if NQCA > 1 else sG.v), zz=sZZ.v):
                    v.wait_ge(sG.h, gg)
                    v.wait_ge(sZZ.h, zz)
                block.vector(ln20_gate_v)

                def ln20_gate_t(t, se=LAST_EXP):
                    t.wait_ge(sS.h, se)
                block.tensor(ln20_gate_t)

                ln2_chunks = []
                for c in range(8):
                    wt = ln2_chunks[c - 1] if c >= 1 else last_stats_mm[0]
                    ln2_chunks.append(ln_chunk(c, 0, True, wt, sq_on_v=True))
                last_stats_mm[0] = ln2_chunks[-1]
                ln2_apps = ln_finish(0, ydst, ln2_chunks[-1],
                                     extra_wait_t=LAST_CTX, p3=True)

        if n_stages >= 2:
            def kv_load(g, gg=sG.v, tt=P2_QR_FREE, pw=sW.v):
                if tt:
                    g.wait_ge(sT.h, tt)
                g.dma_start(out=QR[:].rearrange("p (c o) -> p c o", c=4),
                            in_=ffw_h[0:512, :].rearrange("(c p) o -> p c o", p=128)).then_inc(sW2.h, sW2.bump(16))
                if gg:
                    g.wait_ge(sG.h, gg)
                g.wait_ge(sW.h, pw)
                g.dma_start(out=CX[:].rearrange("p (c o) -> p c o", c=4),
                            in_=ffw_h[512:1024, :].rearrange("(c p) o -> p c o", p=128)).then_inc(sW.h, sW.bump(16))
            block.gpsimd(kv_load)
            KV_QR = sW2.v
            KV_LOADED = sW.v

            sign_vals = []
            for sc_ in range(8):
                def ff_sign(s, sc_=sc_, tgt=(KV_QR if sc_ < 4 else KV_LOADED)):
                    if sc_ < 4:
                        s.wait_ge(sW2.h, tgt)
                    else:
                        s.wait_ge(sW.h, tgt)
                    base = QR if sc_ < 4 else CX
                    k = sc_ % 4
                    s.activation(base[:, k * 1024:k * 1024 + 1024],
                                 base[:, k * 1024:k * 1024 + 1024], AF.Sign).then_inc(sS.h, sS.bump(1))
                block.scalar(ff_sign)
                sign_vals.append(sS.v)
            FF_SIGN = sS.v

            barrier()

            def fslice(kc, oc):
                base = QR if kc < 4 else CX
                k = kc % 4
                return base[:, k * 1024 + oc * 128:k * 1024 + oc * 128 + 128]

            KRAW = F1[:, 1024:1536]
            KSW = F1[:, 512:1024]
            fev_hist = [0, 0]
            krot_hist = [0, 0]
            vevb_hist = [0, 0]
            unit_vb = 0
            kv_live = (n_stages >= 3)
            for tb in range(TB):
                ln3_chunks = []
                fevs = []
                for oc in range(8):
                    b = 2 + (oc % 2)

                    def fmm(t, oc=oc, b=b, apps=tuple(ln2_apps), sgn=tuple(sign_vals),
                            lf=fev_hist[0], tb_=tb):
                        if lf:
                            t.wait_ge(sV.h, lf)
                        for kc in range(8):
                            if oc == 0:
                                if tb_ == 0:
                                    t.wait_ge(sS.h, sgn[kc])
                                t.wait_ge(sV.h, apps[kc])
                            mi = t.matmul(bank(b), fslice(kc, oc),
                                     R1[:, kc * 512:(kc + 1) * 512], start=(kc == 0), stop=(kc == 7))
                        mi.then_inc(sT.h, sT.bump(1))
                    block.tensor(fmm)
                    FMM = sT.v

                    def fev(v, oc=oc, b=b, tb=tb, tgt=FMM):
                        v.wait_ge(sT.h, tgt)
                        xsl = X[:, oc * S + tb * 512:oc * S + tb * 512 + 512]
                        v.tensor_add(xsl, xsl, bank(b)).then_inc(sV.h, sV.bump(1))
                    block.vector(fev)
                    fev_hist = [fev_hist[1], sV.v]
                    fevs.append(sV.v)
                    if kv_live and oc >= 1:
                        wt3 = ln3_chunks[oc - 2] if oc >= 2 else last_stats_mm[0]
                        ln3_chunks.append(ln_chunk(oc - 1, tb * 512, True, wt3, sqw_v=fevs[oc - 1]))
                FFN_T = sT.v

                if not kv_live:
                    if tb + 1 < TB:
                        ln2_apps = [emit_ln((tb + 1) * 512, ydst, p3=True)] * 8
                    continue
                ln3_chunks.append(ln_chunk(7, tb * 512, True, ln3_chunks[-1], sqw_v=fevs[7]))
                last_stats_mm[0] = ln3_chunks[-1]
                ln3_apps = ln_finish(tb * 512, ydst, ln3_chunks[-1], extra_wait_t=FFN_T, p3=True)
                ln3_done = ln3_apps[-1]
                ln2_chunks = []
                for oc in range(8):
                    b = 4 + (oc % 2)

                    def kmm(t, oc=oc, b=b, apps=tuple(ln3_apps), lk=krot_hist[0]):
                        if lk:
                            t.wait_ge(sV.h, lk)
                        for kc in range(8):
                            if oc == 0:
                                t.wait_ge(sV.h, apps[kc])
                            mi = t.matmul(bank(b), E[:, kc * 2048 + oc * 128:kc * 2048 + oc * 128 + 128],
                                     R1[:, kc * 512:(kc + 1) * 512], start=(kc == 0), stop=(kc == 7))
                        mi.then_inc(sT.h, sT.bump(1))
                    block.tensor(kmm)
                    KMM = sT.v
                    if tb == TB - 1 and oc == 7:
                        def qw_load(g, kt=KMM, pw=sW.v):
                            g.wait_ge(sT.h, kt)
                            g.wait_ge(sW.h, pw)
                            g.dma_start(out=E[:, 0:16384].rearrange("p (c o) -> p c o", c=8)[:, :, 0:1024],
                                        in_=qw_h.rearrange("(c p) o -> p c o", p=128)).then_inc(sW.h, sW.bump(16))
                        block.gpsimd(qw_load)
                        QW_Q = sW.v

                    def kev(s, oc=oc, b=b, tgt=KMM, pk=krot_hist[1]):
                        s.wait_ge(sT.h, tgt)
                        if pk:
                            s.wait_ge(sV.h, pk)
                        s.activation(KRAW, bank(b), AF.Identity,
                                     bias=QKVB[:, 8 + oc:8 + oc + 1]).then_inc(sS.h, sS.bump(1))
                    block.scalar(kev)
                    KEV = sS.v

                    def ksw_g(y, tgt=KEV):
                        y.wait_ge(sS.h, tgt)
                        y.dma_start(out=KSW[0:32, :], in_=KRAW[32:64, :]).then_inc(sZS.h, sZS.bump(16))
                        y.dma_start(out=KSW[32:64, :], in_=KRAW[0:32, :]).then_inc(sZS.h, sZS.bump(16))
                        y.dma_start(out=KSW[64:96, :], in_=KRAW[96:128, :]).then_inc(sZS.h, sZS.bump(16))
                        y.dma_start(out=KSW[96:128, :], in_=KRAW[64:96, :]).then_inc(sZS.h, sZS.bump(16))
                    block.sync(ksw_g)
                    KSWD = sZS.v

                    def krot(v, oc=oc, tb=tb, tgt=KSWD, rz=sZ.v):
                        v.wait_ge(sZS.h, tgt)
                        v.wait_ge(sZ.h, rz)
                        tsl = slice(tb * 512, tb * 512 + 512)
                        v.tensor_mul(KRAW, KRAW, RC[:, tsl]).then_inc(sV.h, sV.bump(1))
                        v.tensor_mul(KSW, KSW, RS[:, tsl]).then_inc(sV.h, sV.bump(1))
                        v.wait_ge(sV.h, sV.v)
                        v.tensor_add(KB[:, oc * S + tb * 512:oc * S + tb * 512 + 512], KRAW, KSW).then_inc(sV.h, sV.bump(1))
                    block.vector(krot)
                    krot_hist = [krot_hist[1], sV.v]
                KV_T = sT.v

                for u in range(4):
                    tc = tb * 4 + u
                    for oh in range(2):
                        bvb = 6 + (unit_vb % 2)

                        def vmmb(t, u=u, oh=oh, bvb=bvb, vt=ln3_done, lv=vevb_hist[0], qb=QBVB_DONE):
                            t.wait_ge(sV.h, max(vt, lv, qb))
                            for c in range(8):
                                mi = t.matmul(bank(bvb), R1[:, c * 512 + u * 128:c * 512 + u * 128 + 128],
                                         E[:, c * 2048 + 1024 + oh * 512:c * 2048 + 1024 + oh * 512 + 512],
                                         start=(c == 0), stop=False)
                            mi = t.matmul(bank(bvb), ONESR[:], QBVB[:, oh * 512:oh * 512 + 512],
                                     start=False, stop=True)
                            mi.then_inc(sT.h, sT.bump(1))
                        block.tensor(vmmb)
                        VMMB = sT.v

                        def vevb(v, tc=tc, oh=oh, bvb=bvb, tgt=VMMB):
                            v.wait_ge(sT.h, tgt)
                            dst = VT[:, tc * 1040 + oh * 520:tc * 1040 + oh * 520 + 520]
                            dst = dst.rearrange("p (h c) -> p h c", c=65)[:, :, 0:64]
                            v.tensor_copy(dst, bank(bvb)).then_inc(sV.h, sV.bump(1))
                        block.vector(vevb)
                        vevb_hist = [vevb_hist[1], sV.v]
                        unit_vb += 1
                        if tb + 1 < TB and len(ln2_chunks) < 8:
                            oc2 = len(ln2_chunks)
                            wt2 = ln2_chunks[oc2 - 1] if oc2 >= 1 else last_stats_mm[0]
                            ln2_chunks.append(ln_chunk(oc2, (tb + 1) * 512, True, wt2))

                if tb + 1 < TB:
                    last_stats_mm[0] = ln2_chunks[-1]
                    ln2_apps = ln_finish((tb + 1) * 512, ydst, ln2_chunks[-1],
                                         extra_wait_t=sT.v, p3=True)
                else:
                    def pjw_load(g, vt=sT.v, pw=sW.v):
                        g.wait_ge(sT.h, vt)
                        g.wait_ge(sW.h, pw)
                        g.dma_start(out=E[:, 0:16384].rearrange("p (c o) -> p c o", c=8)[:, :, 1024:2048],
                                    in_=pjw_h.rearrange("(c p) o -> p c o", p=128)).then_inc(sW.h, sW.bump(16))
                    block.gpsimd(pjw_load)
                    QW_ALL = sW.v

            barrier()

        if n_stages >= 3:
            barrier()

            def tri_build(v):
                v.tensor_scalar(STAIR[:], STAIR[:], -0.5, None, ALU.is_ge).then_inc(sV.h, sV.bump(1))
                v.wait_ge(sV.h, sV.v)
            block.vector(tri_build)
            CT2 = F1[0:65, 0:1024]
            ZRIB = F1[0:1, 2048:2560].bitcast(BF16)
            MEANb = F1[:, 1024:1536]
            RSTDb = F1[:, 1536:2048]
            QRAWb = F1[:, 2048:2560]
            QSWb = F1[:, 2560:3072]
            XBFb = QR[:, 3072:3584]
            SQBb = QR[:, 3584:4096]

            def ln_p4_chunk(c, toff, wt, wz=0):
                xsl = X[:, c * S + toff:c * S + toff + 512]

                def cst(v, xsl=xsl, wt=wt, c=c, wz=wz):
                    if wt:
                        v.wait_ge(sT.h, wt)
                    if c == 0 and wz:
                        v.wait_ge(sZF.h, wz)
                    v.tensor_copy(XBFb, xsl).then_inc(sV.h, sV.bump(1))
                    v.wait_ge(sV.h, sV.v)
                    v.tensor_mul(SQBb, XBFb, XBFb).then_inc(sV.h, sV.bump(1))
                block.vector(cst)
                CD = sV.v

                def smm(t, c=c, vt=CD):
                    t.wait_ge(sV.h, vt)
                    mi = t.matmul(bank(4), ONESQ[:], XBFb, start=(c == 0), stop=(c == 7))
                    mi = t.matmul(bank(5), ONESQ[:], SQBb, start=(c == 0), stop=(c == 7)).then_inc(sT.h, sT.bump(1))
                block.tensor(smm)
                return sT.v

            def emit_ln_p4(toff, ydst_fn, first_wait_t=0, wait_zf=0):
                smm_list = []
                for c in range(8):
                    wt = smm_list[c - 1] if c >= 1 else first_wait_t
                    smm_list.append(ln_p4_chunk(c, toff, wt, wait_zf))
                SMM_ALL = smm_list[-1]

                return ln_p4_finish(toff, ydst_fn, SMM_ALL)

            def ln_p4_finish(toff, ydst_fn, SMM_ALL, ew_t=0, ew_zf=0):
                def stv(v, tgt=SMM_ALL):
                    v.wait_ge(sT.h, tgt)
                    v.tensor_scalar_mul(MEANb, bank(4), 1.0 / 1024.0).then_inc(sV.h, sV.bump(1))
                    v.wait_ge(sV.h, sV.v)
                    v.tensor_mul(RSTDb, MEANb, MEANb).then_inc(sV.h, sV.bump(1))
                    v.wait_ge(sV.h, sV.v)
                    v.scalar_tensor_tensor(RSTDb, bank(5), 1.0 / 1024.0, RSTDb,
                                           ALU.mult, ALU.subtract).then_inc(sV.h, sV.bump(1))
                    v.wait_ge(sV.h, sV.v)
                    v.tensor_scalar_add(RSTDb, RSTDb, 1e-5).then_inc(sV.h, sV.bump(1))
                    v.wait_ge(sV.h, sV.v)
                block.vector(stv)
                VARD = sV.v

                def sts(s, tgt=VARD):
                    s.wait_ge(sV.h, tgt)
                    s.activation(RSTDb, RSTDb, AF.Ln).then_inc(sS.h, sS.bump(1))
                    s.wait_ge(sS.h, sS.v)
                    s.activation(RSTDb, RSTDb, AF.Exp, scale=-0.5).then_inc(sS.h, sS.bump(1))
                block.scalar(sts)
                RSD = sS.v

                appv = []
                for c in range(8):
                    xsl = X[:, c * S + toff:c * S + toff + 512]

                    def app(v, xsl=xsl, c=c, dst=ydst_fn(c), ew_t=ew_t, ew_zf=ew_zf):
                        if c == 0:
                            v.wait_ge(sS.h, RSD)
                            if ew_t:
                                t_ = ew_t
                                v.wait_ge(sT.h, t_)
                            if ew_zf:
                                v.wait_ge(sZF.h, ew_zf)
                        v.scalar_tensor_tensor(QRAWb, MEANb, -1.0, xsl, ALU.mult, ALU.add).then_inc(sV.h, sV.bump(1))
                        v.wait_ge(sV.h, sV.v)
                        v.tensor_mul(dst, QRAWb, RSTDb).then_inc(sV.h, sV.bump(1))
                        v.wait_ge(sV.h, sV.v)
                    block.vector(app)
                    appv.append(sV.v)
                return appv

            prev_exp_b = 0
            prev_evb = 0
            prev_ctxb = 0
            prev_zbf = 0
            prev_shift_z = 0
            prev_nrb_v = 0
            nrb_slot = {}
            shf_slot = {}
            pending_tail_b = [None]
            NQC = S // 512
            for qc in range(NQC):
                p4_apps = emit_ln_p4(qc * 512, ydst, first_wait_t=prev_zbf, wait_zf=prev_shift_z)

                qev_hist = [0, 0]
                qrot_chunk = {}
                QRAWq = [QRAWb, CX[:, 0:1024].bitcast(F32)]
                QSWq = [QSWb, CX[:, 1024:2048].bitcast(F32)]
                for oc in range(8):
                    b = 6 + (oc % 2)
                    QRW = QRAWq[oc % 2]
                    QSW_ = QSWq[oc % 2]

                    def qmm(t, oc=oc, b=b, apps=tuple(p4_apps), lq=qev_hist[0]):
                        if lq:
                            t.wait_ge(sV.h, lq)
                        if oc == 0:
                            t.wait_ge(sW.h, QW_Q)
                        for kc in range(8):
                            if oc == 0:
                                t.wait_ge(sV.h, apps[kc])
                            mi = t.matmul(bank(b), E[:, kc * 2048 + oc * 128:kc * 2048 + oc * 128 + 128],
                                     R1[:, kc * 512:(kc + 1) * 512], start=(kc == 0), stop=(kc == 7))
                        mi.then_inc(sT.h, sT.bump(1))
                    block.tensor(qmm)
                    QMM = sT.v
                    LAST_QMM = QMM

                    def qev(s, oc=oc, b=b, tgt=QMM, QRW=QRW, pq=qev_hist[1]):
                        s.wait_ge(sT.h, tgt)
                        if pq:
                            s.wait_ge(sV.h, pq)
                        s.activation(QRW, bank(b), AF.Identity,
                                     bias=QKVB[:, oc:oc + 1]).then_inc(sS.h, sS.bump(1))
                    block.scalar(qev)
                    QEV = sS.v

                    def qsw_g(y, tgt=QEV, QRW=QRW, QSW_=QSW_):
                        y.wait_ge(sS.h, tgt)
                        y.dma_start(out=QSW_[0:32, :], in_=QRW[32:64, :]).then_inc(sZS.h, sZS.bump(16))
                        y.dma_start(out=QSW_[32:64, :], in_=QRW[0:32, :]).then_inc(sZS.h, sZS.bump(16))
                        y.dma_start(out=QSW_[64:96, :], in_=QRW[96:128, :]).then_inc(sZS.h, sZS.bump(16))
                        y.dma_start(out=QSW_[96:128, :], in_=QRW[64:96, :]).then_inc(sZS.h, sZS.bump(16))
                    block.sync(qsw_g)
                    QSWD = sZS.v

                    def qrot(v, oc=oc, qc=qc, tgt=QSWD, rz=sZ.v, QRW=QRW, QSW_=QSW_):
                        v.wait_ge(sZS.h, tgt)
                        v.wait_ge(sZ.h, rz)
                        tsl = slice(qc * 512, qc * 512 + 512)
                        v.tensor_mul(QRW, QRW, RC[:, tsl]).then_inc(sV.h, sV.bump(1))
                        v.tensor_mul(QSW_, QSW_, RS[:, tsl]).then_inc(sV.h, sV.bump(1))
                        v.wait_ge(sV.h, sV.v)
                        v.tensor_add(QR[:, oc * 512:oc * 512 + 512], QRW, QSW_).then_inc(sV.h, sV.bump(1))
                    block.vector(qrot)
                    qev_hist = [qev_hist[1], sV.v]
                    qrot_chunk[oc] = sV.v
                QROT_DONE = sV.v
                if qc == NQC - 1 and n_stages >= 4:
                    def ff2_load(g, kt=LAST_QMM, pw=sW.v):
                        g.wait_ge(sT.h, kt)
                        g.wait_ge(sW.h, pw)
                        g.dma_start(out=E[:, 0:16384].rearrange("p (c o) -> p c o", c=8)[:, :, 0:1024],
                                    in_=ffw_h.rearrange("(c p) o -> p c o", p=128)).then_inc(sW.h, sW.bump(16))
                    block.gpsimd(ff2_load)
                    FF2_W = sW.v
                    sign2_vals = []

                exp_done_qc = {}
                ctx_done_qc = {}
                for gidx in range(8):
                    h0, h1 = 2 * gidx, 2 * gidx + 1
                    ntc = 4 * qc + 4

                    def emit_scores_b(tc, hi_, h):
                        u = tc - 4 * qc
                        wv = exp_done_qc.get((gidx, tc - 1, hi_), prev_exp_b)
                        r0 = 64 * (h % 2)

                        def sc(t, tc=tc, hi_=hi_, h=h, u=u, wv=wv, r0=r0, qr=qrot_chunk[h // 2]):
                            if wv:
                                t.wait_ge(sS.h, wv)
                            if tc == 0:
                                t.wait_ge(sV.h, qr)
                            co = 128 * max(0, u)
                            lhsT = KB[r0:r0 + 64, (h // 2) * S + tc * 128:(h // 2) * S + tc * 128 + 128]
                            rhs = QR[r0:r0 + 64, (h // 2) * 512 + co:(h // 2) * 512 + 512]
                            mi = t.matmul(PS[:, hi_ * 512 + co:hi_ * 512 + 512], lhsT, rhs,
                                          start=True, stop=True,
                                          tile_position=(r0, 0))
                            mi.then_inc(sT.h, sT.bump(1))
                        block.tensor(sc)
                        return sT.v

                    def emit_exp_b(tc):
                        buf = tc % 2
                        sct = scb_done[(tc, 1)]
                        bufree = max(ctx_done_qc.get((gidx, tc - 2, 0), prev_ctxb),
                                     ctx_done_qc.get((gidx, tc - 2, 1), prev_ctxb))

                        co = 128 * max(0, tc - 4 * qc)

                        def ex(s, tc=tc, buf=buf, sct=sct, bufree=bufree, co=co):
                            s.wait_ge(sT.h, max(sct, bufree))
                            if co == 0:
                                s.activation(R1[:, buf * 1024:buf * 1024 + 1024],
                                             PS[:, 0:1024],
                                             AF.Exp, scale=0.125).then_inc(sS.h, sS.bump(1))
                            else:
                                s.activation(R1[:, buf * 1024 + co:buf * 1024 + 512],
                                             PS[:, co:512],
                                             AF.Exp, scale=0.125).then_inc(sS.h, sS.bump(1))
                                s.activation(R1[:, buf * 1024 + 512 + co:buf * 1024 + 1024],
                                             PS[:, 512 + co:1024],
                                             AF.Exp, scale=0.125).then_inc(sS.h, sS.bump(1))
                        block.scalar(ex)
                        exp_done_qc[(gidx, tc, 0)] = sS.v
                        exp_done_qc[(gidx, tc, 1)] = sS.v
                        EXPV = sS.v
                        u = tc - 4 * qc
                        out = {0: [("s", EXPV)], 1: [("s", EXPV)]}
                        if u >= 0:
                            for hi_ in (0, 1):
                                def tmsk(v, buf=buf, hi_=hi_, u=u, tgt=EXPV):
                                    v.wait_ge(sS.h, tgt)
                                    sl = R1[:, buf * 1024 + hi_ * 512 + 128 * u:buf * 1024 + hi_ * 512 + 128 * u + 128]
                                    v.tensor_mul(sl, sl, STAIR[:]).then_inc(sV.h, sV.bump(1))
                                block.vector(tmsk)
                                out[hi_].append(("v", sV.v))
                        return out

                    def emit_ctx_b(tc, hi_, h, rdy):
                        buf = tc % 2

                        def cx(t, tc=tc, hi_=hi_, h=h, buf=buf, rdy=tuple(rdy),
                               pe=prev_evb, ntc=ntc):
                            for kind, val in rdy:
                                if kind == "g":
                                    t.wait_ge(sG.h, val)
                                elif kind == "v":
                                    t.wait_ge(sV.h, val)
                                else:
                                    t.wait_ge(sS.h, val)
                            if tc == 0 and pe:
                                t.wait_ge(sV.h, pe)
                            lhsT = VT[:, tc * 1040 + 65 * h:tc * 1040 + 65 * h + 65]
                            co = 128 * max(0, tc - 4 * qc)
                            mi = t.matmul(PS[0:65, 1024 + hi_ * 512 + co:1024 + hi_ * 512 + 512],
                                          lhsT,
                                          R1[:, buf * 1024 + hi_ * 512 + co:buf * 1024 + hi_ * 512 + 512],
                                          start=(tc == 0), stop=(tc == ntc - 1))
                            mi.then_inc(sT.h, sT.bump(1))
                        block.tensor(cx)
                        ctx_done_qc[(gidx, tc, hi_)] = sT.v

                    scb_done = {}
                    rdy_map = {}
                    for tc in range(ntc):
                        for hi_, h in ((0, h0), (1, h1)):
                            scb_done[(tc, hi_)] = emit_scores_b(tc, hi_, h)
                        out = emit_exp_b(tc)
                        rdy_map[(tc, 0)] = out[0]
                        rdy_map[(tc, 1)] = out[1]
                        if tc > 0:
                            for hi_, h in ((0, h0), (1, h1)):
                                emit_ctx_b(tc - 1, hi_, h, rdy_map[(tc - 1, hi_)])
                    for hi_, h in ((0, h0), (1, h1)):
                        emit_ctx_b(ntc - 1, hi_, h, rdy_map[(ntc - 1, hi_)])
                    prev_exp_b = exp_done_qc[(gidx, ntc - 1, 1)]
                    CTXB = sT.v
                    prev_ctxb = CTXB

                    if pending_tail_b[0] is not None:
                        pending_tail_b[0]()
                        pending_tail_b[0] = None

                    if qc == NQC - 1 and n_stages >= 4 and gidx >= 2:
                        for _ in range(2):
                            if len(sign2_vals) < 8:
                                sc_ = len(sign2_vals)

                                def ff2_sign(s, sc_=sc_, tgt=FF2_W):
                                    s.wait_ge(sW.h, tgt)
                                    sl = E[:, sc_ * 2048:sc_ * 2048 + 1024]
                                    s.activation(sl, sl, AF.Sign).then_inc(sS.h, sS.bump(1))
                                block.scalar(ff2_sign)
                                sign2_vals.append(sS.v)

                    def evb(v, tgt=CTXB, pvv=sV.v):
                        v.wait_ge(sT.h, tgt)
                        v.wait_ge(sV.h, pvv)
                        v.tensor_copy(CT2[:, 0:1024], PS[0:65, 1024:2048]).then_inc(sV.h, sV.bump(1))
                    block.vector(evb)
                    EVB = sV.v
                    prev_evb = EVB

                    def zmb(y, tgt=EVB):
                        y.wait_ge(sV.h, tgt)
                        y.dma_start(out=ZM[:, 0:8], in_=F1[64:65, 0:1024]).then_inc(sZZ.h, sZZ.bump(16))
                    block.sync(zmb)
                    ZMB = sZZ.v

                    def rcb(v, tgt=ZMB):
                        v.wait_ge(sZZ.h, tgt)
                        v.reciprocal(ZMI[:, 0:8], ZM[:, 0:8]).then_inc(sV.h, sV.bump(1))
                    block.vector(rcb)
                    RCB = sV.v

                    def zrb(g, tgt=RCB, pz=prev_zbf):
                        g.wait_ge(sV.h, tgt)
                        if pz:
                            g.wait_ge(sT.h, pz)
                        g.dma_start(out=ZRIB, in_=ZMI[:, 0:8]).then_inc(sZI.h, sZI.bump(16))
                    block.gpsimd(zrb)
                    ZRB = sZI.v

                    def make_tail_b(ZRB=ZRB, gidx=gidx):
                        def emit_tail_b():
                            nonlocal prev_zbf, prev_nrb_v, prev_shift_z
                            for hi_ in range(2):
                                zb_b = 4 + hi_

                                def zbf(t, hi_=hi_, zb_b=zb_b, tgt=ZRB):
                                    t.wait_ge(sZI.h, tgt)
                                    mi = t.matmul(bank(zb_b, rows=64), ONESR[0:1, 0:64],
                                                  ZRIB[0:1, hi_ * 512:hi_ * 512 + 512], start=True, stop=True)
                                    mi.then_inc(sT.h, sT.bump(1))
                                block.tensor(zbf)
                                ZBF = sT.v
                                prev_zbf = ZBF

                                h_ = 2 * gidx + hi_
                                if h_ % 2 == 0:
                                    def nrb(v, hi_=hi_, h=h_, zb_b=zb_b, tgt=ZBF):
                                        v.wait_ge(sT.h, tgt)
                                        dst = CX[0:64, (h // 2) * 512:(h // 2) * 512 + 512]
                                        v.tensor_tensor(dst, bank(zb_b, rows=64),
                                                        CT2[0:64, hi_ * 512:hi_ * 512 + 512], ALU.mult).then_inc(sV.h, sV.bump(1))
                                    block.vector(nrb)
                                    prev_nrb_v = sV.v
                                    nrb_slot[h_ // 2] = sV.v
                                else:
                                    def nrb(v, hi_=hi_, zb_b=zb_b, tgt=ZBF, pshf=prev_shift_z):
                                        v.wait_ge(sT.h, tgt)
                                        if pshf:
                                            v.wait_ge(sZF.h, pshf)
                                        v.tensor_tensor(R1[0:64, 2048:2560], bank(zb_b, rows=64),
                                                        CT2[0:64, hi_ * 512:hi_ * 512 + 512], ALU.mult).then_inc(sV.h, sV.bump(1))
                                    block.vector(nrb)
                                    NRBv = sV.v
                                    prev_nrb_v = NRBv

                                    def shf(y3, h=h_, tgt=NRBv):
                                        y3.wait_ge(sV.h, tgt)
                                        y3.dma_start(out=CX[64:128, (h // 2) * 512:(h // 2) * 512 + 512],
                                                    in_=R1[0:64, 2048:2560]).then_inc(sZF.h, sZF.bump(16))
                                    block.sync(shf)
                                    prev_shift_z = sZF.v
                                    shf_slot[h_ // 2] = sZF.v
                        return emit_tail_b

                    pending_tail_b[0] = make_tail_b()

                for oc in range(2):
                    def pmm_p1(t, oc=oc, qc_=qc, pe=prev_exp_b,
                               ns=tuple(nrb_slot.get(k, 0) for k in range(7)),
                               ss=tuple(shf_slot.get(k, 0) for k in range(7))):
                        t.wait_ge(sS.h, pe)
                        if oc == 0 and qc_ == 0:
                            t.wait_ge(sW.h, QW_ALL)
                        for kc in range(7):
                            if oc == 0:
                                if ns[kc]:
                                    t.wait_ge(sV.h, ns[kc])
                                if ss[kc]:
                                    t.wait_ge(sZF.h, ss[kc])
                            mi = t.matmul(bank(oc), E[:, kc * 2048 + 1024 + oc * 128:kc * 2048 + 1024 + oc * 128 + 128],
                                     CX[:, kc * 512:(kc + 1) * 512], start=(kc == 0), stop=False)
                        mi.then_inc(sT.h, sT.bump(1))
                    block.tensor(pmm_p1)

                if pending_tail_b[0] is not None:
                    pending_tail_b[0]()
                    pending_tail_b[0] = None
                CXC_DONE = sV.v

                pev_hist = [0, 0]
                for oc in range(8):
                    b = oc % 2

                    if oc < 2:
                        def pmm(t, oc=oc, b=b, n7=nrb_slot[7], s7=shf_slot[7]):
                            t.wait_ge(sV.h, n7)
                            t.wait_ge(sZF.h, s7)
                            mi = t.matmul(bank(b), E[:, 7 * 2048 + 1024 + oc * 128:7 * 2048 + 1024 + oc * 128 + 128],
                                     CX[:, 7 * 512:8 * 512], start=False, stop=True)
                            mi.then_inc(sT.h, sT.bump(1))
                        block.tensor(pmm)
                    else:
                        def pmm(t, oc=oc, b=b, vt=CXC_DONE, lp=pev_hist[0], sz=prev_shift_z):
                            t.wait_ge(sV.h, max(vt, lp))
                            if sz:
                                t.wait_ge(sZF.h, sz)
                            for kc in range(8):
                                mi = t.matmul(bank(b), E[:, kc * 2048 + 1024 + oc * 128:kc * 2048 + 1024 + oc * 128 + 128],
                                         CX[:, kc * 512:(kc + 1) * 512], start=(kc == 0), stop=(kc == 7))
                            mi.then_inc(sT.h, sT.bump(1))
                        block.tensor(pmm)
                    PMM = sT.v

                    def pev(v, oc=oc, b=b, qc=qc, tgt=PMM):
                        v.wait_ge(sT.h, tgt)
                        xsl = X[:, oc * S + qc * 512:oc * S + qc * 512 + 512]
                        v.scalar_tensor_tensor(xsl, bank(b), PROJB[:, oc:oc + 1], xsl,
                                               ALU.add, ALU.add).then_inc(sV.h, sV.bump(1))
                    block.vector(pev)
                    pev_hist = [pev_hist[1], sV.v]

        if n_stages >= 4:
            barrier()

            fev2_hist = [0, 0]
            fev2_oc = {}
            for tb in range(TB):
                ln_done = emit_ln(tb * 512, ydst)
                for oc in range(8):
                    b = 2 + (oc % 2)

                    def fmm2(t, oc=oc, b=b, vt=ln_done, sgn=tuple(sign2_vals),
                             lf=fev2_hist[0], tb_=tb):
                        t.wait_ge(sV.h, max(vt, lf))
                        for kc in range(8):
                            if oc == 0 and tb_ == 0:
                                t.wait_ge(sS.h, sgn[kc])
                            mi = t.matmul(bank(b), E[:, kc * 2048 + oc * 128:kc * 2048 + oc * 128 + 128],
                                     R1[:, kc * 512:(kc + 1) * 512], start=(kc == 0), stop=(kc == 7))
                        mi.then_inc(sT.h, sT.bump(1))
                    block.tensor(fmm2)
                    FMM2 = sT.v

                    def fev2(v, oc=oc, b=b, tb=tb, tgt=FMM2):
                        v.wait_ge(sT.h, tgt)
                        xsl = X[:, oc * S + tb * 512:oc * S + tb * 512 + 512]
                        v.tensor_add(xsl, xsl, bank(b)).then_inc(sV.h, sV.bump(1))
                    block.vector(fev2)
                    fev2_hist = [fev2_hist[1], sV.v]
                    fev2_oc[oc] = sV.v
                if tb < TB - 1:
                    def out_tb(y, tb=tb, vt=sV.v):
                        y.wait_ge(sV.h, vt)
                        y.dma_start(out=out_h.rearrange("(c p) t -> p c t", p=128)[:, :, tb * 512:tb * 512 + 512],
                                    in_=X[:].rearrange("p (c t) -> p c t", c=8)[:, :, tb * 512:tb * 512 + 512]).then_inc(sO.h, sO.bump(16))
                    block.sync(out_tb)
                else:
                    for oc2 in range(8):
                        def out_oc(y, tb=tb, oc2=oc2, vt=fev2_oc[oc2]):
                            y.wait_ge(sV.h, vt)
                            y.dma_start(out=out_h.rearrange("(c p) t -> p c t", p=128)[:, oc2, tb * 512:tb * 512 + 512],
                                        in_=X[:, oc2 * S + tb * 512:oc2 * S + tb * 512 + 512]).then_inc(sO.h, sO.bump(16))
                        block.sync(out_oc)

        barrier()
        tfin, sfin, vfin, gfin = sT.v, sS.v, sV.v, sG.v

        def out_y(y, done=sO.v):
            if tfin: y.wait_ge(sT.h, tfin)
            if sfin: y.wait_ge(sS.h, sfin)
            if vfin: y.wait_ge(sV.h, vfin)
            if gfin: y.wait_ge(sG.h, gfin)
            if done == 0:
                y.dma_start(out=out_h.rearrange("(c p) t -> p c t", p=128), in_=X[:].rearrange("p (c t) -> p c t", c=8)).then_inc(sO.h, sO.bump(16))
            y.wait_ge(sO.h, sO.v)
        block.sync(out_y)

    es.close()
    return nc


def _prep_core_inputs(xb, Wq, bq, Wqc, bqc, Wk, bkv, Wkc, bkc, Wv, bv, Wvc, bvc,
                      qkv_w, qkv_b, proj_w, proj_b, ffn_w, S):
    f = np.float32
    d = {}
    d["x"] = np.ascontiguousarray(xb.T.astype(f))
    perm = np.zeros(256, np.int64)
    for h in range(16):
        g, p = h // 2, h % 2
        r = 64 * p + 32 * (g % 2) + 16 * ((g // 2) % 2)
        c = g // 4
        perm[c * 128 + r:c * 128 + r + 16] = np.arange(16 * h, 16 * h + 16)
    d["wa4"] = np.ascontiguousarray(np.concatenate(
        [Wq[:, perm], Wqc[:, perm], Wk, Wkc], axis=1).astype(f))
    d["wv2"] = np.ascontiguousarray(np.concatenate([Wv, Wvc], axis=1).astype(f))
    qkvT = np.ascontiguousarray(qkv_w.T.astype(f))
    d["kvw"] = np.ascontiguousarray(qkvT[:, 1024:3072])
    d["qw"] = np.ascontiguousarray(qkvT[:, 0:1024])
    d["pjw"] = np.ascontiguousarray(proj_w.T.astype(f))
    d["ffw"] = np.ascontiguousarray(ffn_w.T.astype(f))
    bqp = bq[perm]
    bqcp = bqc[perm]
    d["ba_qk"] = np.ascontiguousarray(
        np.stack([bqp[0:128], bqp[128:256],
                  bqcp[0:128], bqcp[128:256],
                  bkv.reshape(2, 128)[0], bkv.reshape(2, 128)[1],
                  bkc.reshape(2, 128)[0], bkc.reshape(2, 128)[1]], axis=1).astype(f))
    d["bv_row"] = np.ascontiguousarray(np.concatenate([bv, bvc])[None, :].astype(f))
    d["qkvb_qk"] = np.ascontiguousarray(
        np.concatenate([qkv_b[0:1024].reshape(8, 128).T,
                        qkv_b[1024:2048].reshape(8, 128).T], axis=1).astype(f))
    d["qkvb_v"] = np.ascontiguousarray(qkv_b[2048:3072][None, :].astype(f))
    d["projb"] = np.ascontiguousarray(proj_b.reshape(8, 128).T.astype(f))
    bk32 = np.zeros((128, 16), f)
    bkc32 = np.zeros((128, 16), f)
    for h in range(16):
        g, p = h // 2, h % 2
        r = 64 * p + 32 * (g % 2) + 16 * ((g // 2) % 2)
        bk32[r:r + 16, h] = bkv[16 * h:16 * h + 16]
        bkc32[r:r + 16, h] = bkc[16 * h:16 * h + 16]
    d["bk32"] = bk32
    d["bkc32"] = bkc32
    t64 = np.arange(S, dtype=np.float64)
    j = np.arange(64, dtype=np.float64) % 32
    invf = np.exp(-j * math.log(10000.0) / 32.0)
    theta = invf[:, None] * t64[None, :]
    d["tmat"] = np.broadcast_to(t64[None, :], (64, S)).astype(f).copy()
    d["jcol"] = j[:, None].astype(f)
    d["ksin"] = np.round(theta / (2 * math.pi)).astype(f)
    d["kcos"] = np.round((theta + math.pi / 2) / (2 * math.pi)).astype(f)
    st = np.where(np.arange(128)[None, :] < np.arange(128)[:, None],
                  np.float32(NEG), np.float32(0.0))
    d["stair"] = st.astype(ml_dtypes.bfloat16)
    d["ident"] = np.eye(128, dtype=f).astype(ml_dtypes.bfloat16)
    return d


def kernel(L0_attn_a_x, L0_attn_a_Wq, L0_attn_a_bq, L0_attn_a_Wqc, L0_attn_a_bqc,
           L0_attn_a_Wk, L0_attn_a_bk, L0_attn_a_Wkc, L0_attn_a_bkc,
           L0_attn_a_Wv, L0_attn_a_bv, L0_attn_a_Wvc, L0_attn_a_bvc,
           L0_attn_b_qkv_w, L0_attn_b_qkv_b, L0_attn_b_proj_w, L0_attn_b_proj_b,
           L0_ffn_weight):
    x = np.asarray(L0_attn_a_x, dtype=np.float32)
    B, S, _ = x.shape
    nc = build_nc(S=S)
    args = [np.asarray(a, dtype=np.float32) for a in
            (L0_attn_a_Wq, L0_attn_a_bq, L0_attn_a_Wqc, L0_attn_a_bqc,
             L0_attn_a_Wk, L0_attn_a_bk, L0_attn_a_Wkc, L0_attn_a_bkc,
             L0_attn_a_Wv, L0_attn_a_bv, L0_attn_a_Wvc, L0_attn_a_bvc,
             L0_attn_b_qkv_w, L0_attn_b_qkv_b, L0_attn_b_proj_w, L0_attn_b_proj_b,
             L0_ffn_weight)]
    in_maps = [_prep_core_inputs(x[b], *args, S) for b in range(B)]
    res = run_bass_kernel_spmd(nc, in_maps, core_ids=list(range(B)))
    out = np.stack([np.ascontiguousarray(np.asarray(r["out"]).T) for r in res.results], axis=0)
    return out.astype(np.float32)
